# version 40
# baseline (speedup 1.0000x reference)
import numpy as np
import ml_dtypes
import concourse.bass as bass
import concourse.mybir as mybir
from concourse.bass_utils import run_bass_kernel_spmd

F32 = mybir.dt.float32
BF16 = mybir.dt.bfloat16
AF = mybir.ActivationFunctionType
ALU = mybir.AluOpType
AX = mybir.AxisListType.X

NTOK = 2048
DM = 4096
NBLK = 32
F1 = 512
NS1 = 4
NS3 = 8
F3 = 256
C = 512
NCH = 8
NSUB = 4
EPS = 1e-5
BF = ml_dtypes.bfloat16


class Plan:
    ENGINES = ("tensor", "vector", "scalar", "gpsimd", "sync")

    def __init__(self, nc):
        self.nc = nc
        self.ops = {e: [] for e in self.ENGINES}
        self.sems = {}
        self.counts = {}

    def sem(self, name):
        if name not in self.sems:
            self.sems[name] = self.nc.alloc_semaphore(name)
            self.counts[name] = 0
        return self.sems[name]

    def wait(self, eng, name, val=None):
        self.sem(name)
        v = self.counts[name] if val is None else val
        if v <= 0:
            return
        h = self.sems[name]
        self.ops[eng].append(lambda e, h=h, v=v: e.wait_ge(h, v))

    def op(self, eng, fn, inc=None, amt=1):
        if inc is not None:
            h = self.sem(inc)
            self.counts[inc] += amt
            self.ops[eng].append(lambda e, fn=fn, h=h, a=amt: fn(e).then_inc(h, a))
        else:
            self.ops[eng].append(lambda e, fn=fn: fn(e))

    def dma(self, eng, fn, inc, serialize=True):
        if serialize and self.count(inc) > 0:
            self.wait(eng, inc, self.count(inc))
        self.op(eng, fn, inc=inc, amt=16)

    def count(self, name):
        self.sem(name)
        return self.counts[name]

    def emit_into(self, block):
        for eng in self.ENGINES:
            lst = self.ops[eng]
            if not lst:
                continue

            def runner(e, lst=lst):
                for fn in lst:
                    fn(e)
            getattr(block, eng)(runner)


def build_nc():
    nc = bass.Bass("TRN2", target_bir_lowering=False)

    u_in = nc.dram_tensor("u_in", [NS1, 128, NBLK * F1], F32, kind="ExternalInput")
    at_in = nc.dram_tensor("at_in", [DM, 1024], F32, kind="ExternalInput")
    bt_in = nc.dram_tensor("bt_in", [1024, DM], F32, kind="ExternalInput")
    wrt_in = nc.dram_tensor("wrt_in", [DM, 66], F32, kind="ExternalInput")
    lam2 = nc.dram_tensor("lam2", [2, 64], F32, kind="ExternalInput")
    p2 = nc.dram_tensor("p2", [2, 64], F32, kind="ExternalInput")
    q2 = nc.dram_tensor("q2", [2, 64], F32, kind="ExternalInput")
    ct2 = nc.dram_tensor("ct2", [2, 64], F32, kind="ExternalInput")
    vctr = nc.dram_tensor("vctr", [64, 64], F32, kind="ExternalInput")
    vcti = nc.dram_tensor("vcti", [64, 64], F32, kind="ExternalInput")
    bp_in = nc.dram_tensor("bp_in", [64, 1], F32, kind="ExternalInput")
    dd_in = nc.dram_tensor("dd_in", [1, 1], F32, kind="ExternalInput")
    ls_in = nc.dram_tensor("ls_in", [1, 1], F32, kind="ExternalInput")
    identf_in = nc.dram_tensor("identf_in", [128, 128], F32, kind="ExternalInput")
    identb_in = nc.dram_tensor("identb_in", [128, 128], BF16, kind="ExternalInput")
    jrev_in = nc.dram_tensor("jrev_in", [128, 128], F32, kind="ExternalInput")
    onesr_in = nc.dram_tensor("onesr_in", [1, 128], BF16, kind="ExternalInput")
    onesrf_in = nc.dram_tensor("onesrf_in", [1, 128], F32, kind="ExternalInput")
    onesc_in = nc.dram_tensor("onesc_in", [128, 1], BF16, kind="ExternalInput")
    sumw_in = nc.dram_tensor("sumw_in", [128, 128], BF16, kind="ExternalInput")
    repl_in = nc.dram_tensor("repl_in", [64, 1024], BF16, kind="ExternalInput")

    y_out = nc.dram_tensor("y_out", [NS1, 128, NBLK * F1], F32, kind="ExternalOutput")

    xsp_f = nc.dram_tensor("xsp_f", [NS1, 128, NBLK * F1], F32, kind="Internal")
    xsp_b = nc.dram_tensor("xsp_b", [NS1, 128, NBLK * F1], BF16, kind="Internal")
    kp_dram = nc.dram_tensor("kp_dram", [1, 896], F32, kind="Internal")

    off = [16512]

    def alloc(name, shape, dtype, at=None):
        a = off[0] if at is None else at
        h = nc.alloc_sbuf_tensor_at(name, shape, dtype, offset=a)
        if at is None:
            sz = int(np.prod(shape[1:])) * (4 if dtype == F32 else 2)
            off[0] = a + ((sz + 127) // 128) * 128
        return h

    cm_T = alloc("cm_T", [128, 512], BF16)
    cm_ET = alloc("cm_ET", [128, 512], BF16)
    cm_O1T = alloc("cm_O1T", [128, 512], BF16)
    cm_MtT = alloc("cm_MtT", [128, 128], BF16)
    identf = alloc("identf", [128, 128], F32)
    identb = alloc("identb", [128, 128], BF16)
    jrev = alloc("jrev", [128, 128], F32)
    onesr = alloc("onesr", [1, 128], BF16)
    onesrf = alloc("onesrf", [1, 128], F32)
    onesc = alloc("onesc", [128, 1], BF16)
    sumw = alloc("sumw", [128, 128], BF16)
    state = alloc("state", [128, 1024], BF16)
    Rb = alloc("Rb", [128, 1024], F32)
    wrt_b = alloc("wrt_b", [128, NBLK * 66], BF16)
    negmr = alloc("negmr", [1, 1024], BF16)
    nm2 = alloc("nm2", [1, 1024], BF16)
    A1row = alloc("A1row", [1, 1024], BF16)
    W1row = alloc("W1row", [1, 64], BF16)
    C1_OFF = off[0]
    c1row = alloc("c1row", [1, DM], BF16)
    SV_OFF = off[0]
    svec = alloc("svec", [1, 1536], F32)
    rghx = nc.alloc_sbuf_tensor_at("rghx", [128, 1024], BF16, offset=C1_OFF)
    ghx = nc.alloc_sbuf_tensor_at("ghx", [128, 1024], BF16, offset=SV_OFF)
    sc_sb = alloc("sc_sb", [64, 512], F32)
    sT_sb = alloc("sT_sb", [128, 64], F32)
    m8a = alloc("m8a", [128, 8], F32)
    m8b = alloc("m8b", [128, 8], F32)
    e_sb = alloc("e_sb", [128, 64], F32)
    mask_sb = alloc("mask_sb", [128, 64], F32)
    z_sb = alloc("z_sb", [128, 4], F32)
    gt_f = alloc("gt_f", [128, 64], F32)
    r2c = alloc("r2c", [128, 8], F32)
    grep = alloc("grep", [128, 1024], BF16)
    SQS_OFF = off[0]
    sqs = alloc("sqs", [128, 1024], BF16)
    gtf4 = nc.alloc_sbuf_tensor_at("gtf4", [128, 256], F32, offset=SQS_OFF)
    dcol = alloc("dcol", [128, 4], F32)
    htl_b = alloc("htl_b", [128, 4 * NTOK], BF16)

    P_END = off[0]
    RW = ((P_END + 1023) // 1024) * 1024
    assert RW + 131072 + 16384 <= 229344, f"sbuf over: {RW}"

    U0 = alloc("U0", [128, NBLK * F1], F32, at=RW)
    U1 = alloc("U1", [128, NBLK * F1], F32, at=RW + 65536)
    At_sb = alloc("At_sb", [128, NBLK * 1024], BF16, at=RW)
    X2 = [alloc("X2_0", [128, NBLK * F1], BF16, at=RW + 65536),
          alloc("X2_1", [128, NBLK * F1], BF16, at=RW + 65536 + 32768)]
    Bt_lo = alloc("Bt_lo", [128, 4 * DM], BF16, at=RW + 65536)
    Bt_hi = alloc("Bt_hi", [128, 4 * DM], BF16, at=RW)
    XF = [alloc("XF_0", [128, NBLK * F3], F32, at=RW + 32768),
          alloc("XF_1", [128, NBLK * F3], F32, at=RW + 65536 + 32768)]
    LN1 = alloc("ln1w", [128, 8 * 512], BF16, at=RW + 131072)
    UBF = alloc("ubf", [128, 4 * 512], BF16, at=RW + 131072 + 8192)
    cvt = alloc("cvt", [128, 1024], F32, at=RW + 131072 + 12288)
    htl_a = alloc("htl_a", [128, 4 * NTOK], BF16, at=RW + 131072)

    def htl_ap(kr, lo, hi):
        t = htl_a if kr < 4 else htl_b
        k = kr % 4
        return t.ap()[:, k * NTOK + lo:k * NTOK + hi]

    PH0 = RW + 65536

    def ph0(name, shape, dtype, at):
        return nc.alloc_sbuf_tensor_at(name, shape, dtype, offset=PH0 + at)

    vrow = ph0("vrow", [1, 40 * 64], F32, 0)
    smat = ph0("smat", [64, 12 * 64], F32, 10240)
    Sch = [ph0(f"Sch{k}", [128, 128], F32, 14336 + k * 512) for k in range(13)]
    Tch = [ph0(f"Tch{k}", [128, 128], F32, 21504 + k * 512) for k in range(13)]
    Xt = [ph0(f"Xt{k}", [128, 128], F32, 28672 + k * 512) for k in range(6)]
    Vch = ph0("Vch", [128, 640], F32, 31744)
    wtv = ph0("wtv", [128, 4], F32, 34304)
    krow = ph0("krow", [1, 768], F32, 34816)
    kp_sb = ph0("kp_sb", [1, 896], F32, 38400)
    c1stage = ph0("c1stage", [128, 32], F32, 42496)

    psA = nc.alloc_psum_tensor("psA", [128, 2048], F32)
    psS = nc.alloc_psum_tensor("psS", [128, 512], F32)
    ps1 = nc.alloc_psum_tensor("ps1", [128, 512], F32)
    ps2 = nc.alloc_psum_tensor("ps2", [128, 512], F32)
    psX = nc.alloc_psum_tensor("psX", [128, 512], F32)

    P = Plan(nc)
    for _n in ("ld0", "ldg", "uld", "t_s1", "t_sq", "g_sq", "s_ub", "sx", "seq",
               "sqd", "sqg", "c1g", "c1e", "c1t", "c1s", "spf", "spb", "v_ln",
               "s_st", "t_sc", "t_cv", "v_tm", "v_xe", "at1", "at2", "t_h", "btl",
               "bhi", "x2h", "x2l", "t_a1", "v_a1", "s_sco", "t_sco", "v_sq",
               "gq", "t_gr", "s_gp", "v_ht", "xfl", "yd", "t_o", "v_y"):
        P.sem(_n)

    vln_base = {}
    ubc_base = {}
    st_bases = {}
    rb_cnt = {}
    tcv_after = []
    def blk_ap(buf, b, width=F1):
        return buf.ap()[:, b * width:(b + 1) * width]

    M_AP = svec.ap()[0:1, 0:512]
    R_AP = svec.ap()[0:1, 512:1024]
    V_AP = svec.ap()[0:1, 1024:1536]
    UB = [U0, U1]

    vln_base = {}
    ubc_base = {}
    st_bases = {}
    rb_cnt = {}
    tcv_after = []

    def chain(eng, fn):
        P.wait(eng, "sx")
        P.op(eng, fn, inc="sx")

    def uld_full(s):
        return 64 * (s + 1)

    def uld_q(s, b):
        return 64 * s + 16 * (b // 8 + 1)

    def emit_prep1(s):
        buf = UB[s % 2]
        base_s1 = P.count("t_s1")
        base_sq = P.count("t_sq")
        base_gq = P.count("g_sq")
        base_ub = P.count("s_ub")
        sx_prev = P.count("sx")
        steps = []
        for g8 in range(8):
            def step(g8=g8):
                for b in range(4 * g8, 4 * g8 + 4):
                    q = b % 2
                    w = b % 4
                    P.wait("scalar", "uld", uld_q(s, b))
                    if b >= 4:
                        P.wait("scalar", "t_s1", base_s1 + (b - 4) + 1)
                        P.wait("scalar", "g_sq", base_gq + (b - 4) + 1)
                    P.op("scalar", lambda e, b=b, w=w: e.copy(
                        UBF.ap()[:, w * 512:(w + 1) * 512], blk_ap(buf, b)), inc="s_ub")
                    P.wait("tensor", "s_ub", base_ub + b + 1)
                    if b == 0:
                        P.wait("tensor", "sx", sx_prev)
                    P.op("tensor", lambda e, b=b, w=w: e.matmul(
                        ps1.ap()[:, :], sumw.ap()[:, :],
                        UBF.ap()[:, w * 512:(w + 1) * 512],
                        start=(b == 0), stop=(b == NBLK - 1),
                        skip_group_check=True), inc="t_s1")
                    P.wait("gpsimd", "s_ub", base_ub + b + 1)
                    if b >= 2:
                        P.wait("gpsimd", "t_sq", base_sq + (b - 1))
                    P.op("gpsimd", lambda e, b=b, q=q, w=w: e.tensor_mul(
                        sqs.ap()[:, q * 512:(q + 1) * 512],
                        UBF.ap()[:, w * 512:(w + 1) * 512],
                        UBF.ap()[:, w * 512:(w + 1) * 512]), inc="g_sq")
                    P.wait("tensor", "g_sq", base_gq + b + 1)
                    P.op("tensor", lambda e, b=b, q=q: e.matmul(
                        ps2.ap()[:, :], sumw.ap()[:, :],
                        sqs.ap()[:, q * 512:(q + 1) * 512],
                        start=(b == 0), stop=(b == NBLK - 1),
                        skip_group_check=True), inc="t_sq")
            steps.append(step)
        return steps, base_s1, base_sq

    def emit_chain1(s, base_s1, base_sq):
        sl = s % 2
        P.wait("vector", "t_s1", base_s1 + NBLK)
        P.wait("vector", "t_sq", base_sq + NBLK)
        chain("vector", lambda e: e.tensor_scalar(M_AP, ps1.ap()[0:1, :], 1.0 / DM, None, ALU.mult))
        chain("vector", lambda e: e.tensor_scalar(V_AP, ps2.ap()[0:1, :], 1.0 / DM, None, ALU.mult))
        chain("vector", lambda e: e.tensor_mul(R_AP, M_AP, M_AP))
        chain("vector", lambda e: e.tensor_sub(V_AP, V_AP, R_AP))
        chain("vector", lambda e: e.tensor_scalar(V_AP, V_AP, EPS, None, ALU.add))
        chain("scalar", lambda e: e.activation(R_AP, V_AP, AF.Ln))
        chain("scalar", lambda e: e.activation(R_AP, R_AP, AF.Exp, scale=-0.5))
        chain("vector", lambda e, sl=sl: e.tensor_scalar(negmr.ap()[0:1, sl * 512:(sl + 1) * 512], M_AP, -1.0, None, ALU.mult))
        chain("tensor", lambda e: e.matmul(psX.ap()[:, :], onesrf.ap()[:, :], R_AP, start=True, stop=True))
        chain("scalar", lambda e, sl=sl: e.copy(Rb.ap()[:, sl * 512:(sl + 1) * 512], psX.ap()[:, :]))
        rb_cnt[s] = P.count("sx")


    def bload(fn):
        P.dma("sync", fn, "ld0", serialize=False)

    def gload(fn):
        P.dma("gpsimd", fn, "ldg", serialize=False)

    def sd(fn):
        P.wait("sync", "seq")
        P.wait("sync", "sqd")
        P.wait("sync", "sqg")
        P.dma("sync", fn, "sqd")

    def gd(fn):
        P.wait("gpsimd", "seq")
        P.wait("gpsimd", "sqd")
        P.wait("gpsimd", "sqg")
        P.dma("gpsimd", fn, "sqg")

    PRE0 = []
    SO_CTR = [0]

    def so(eng, fn):
        P.wait(eng, "seq")
        P.wait(eng, "sqd")
        P.wait(eng, "sqg")
        P.op(eng, fn, inc="seq")
        SO_CTR[0] += 1
        if SO_CTR[0] % 5 == 0 and SO_CTR[0] <= 45 and PRE0:
            PRE0.pop(0)()

    bload(lambda e: e.dma_start(out=identf.ap(), in_=identf_in.ap()))
    bload(lambda e: e.dma_start(out=identb.ap(), in_=identb_in.ap()))
    bload(lambda e: e.dma_start(out=jrev.ap(), in_=jrev_in.ap()))
    bload(lambda e: e.dma_start(out=onesr.ap(), in_=onesr_in.ap()))
    bload(lambda e: e.dma_start(out=onesrf.ap(), in_=onesrf_in.ap()))
    bload(lambda e: e.dma_start(out=onesc.ap(), in_=onesc_in.ap()))
    bload(lambda e: e.dma_start(out=sumw.ap(), in_=sumw_in.ap()))
    bload(lambda e: e.dma_start(out=rghx.ap()[64:128, :], in_=repl_in.ap()))

    VR = {}

    def vs(name):
        if name not in VR:
            VR[name] = len(VR)
            assert len(VR) <= 36
        i = VR[name]
        return vrow.ap()[0:1, i * 64:(i + 1) * 64]

    SCD = {}

    def ss(name):
        if name not in SCD:
            SCD[name] = len(SCD)
            assert len(SCD) <= 60
        i = SCD[name]
        return vrow.ap()[0:1, 36 * 64 + i:36 * 64 + i + 1]

    for nm, src, row in [("lr", lam2, 0), ("li", lam2, 1), ("pr", p2, 0),
                         ("pi", p2, 1), ("qr", q2, 0), ("qi", q2, 1),
                         ("cr", ct2, 0), ("ci", ct2, 1)]:
        bload(lambda e, d=vs(nm), s=src, r=row: e.dma_start(out=d, in_=s.ap()[r:r + 1, :]))
    bload(lambda e: e.dma_start(out=smat.ap()[0:64, 0:64], in_=vctr.ap()))
    bload(lambda e: e.dma_start(out=smat.ap()[0:64, 64:128], in_=vcti.ap()))
    bload(lambda e: e.dma_start(out=wtv.ap()[0:64, 2:3], in_=bp_in.ap()))
    bload(lambda e: e.dma_start(out=ss("D"), in_=dd_in.ap()))
    bload(lambda e: e.dma_start(out=ss("ls"), in_=ls_in.ap()))
    for db in range(NBLK):
        gload(lambda e, db=db: e.dma_start(out=wrt_b.ap()[:, 66 * db:66 * (db + 1)],
                                           in_=wrt_in.ap()[128 * db:128 * (db + 1), :]))

    ONES64 = vs("o64")
    for eng in ("vector", "scalar", "tensor"):
        P.wait(eng, "ld0", P.count("ld0"))
    for eng in ("sync", "gpsimd"):
        P.wait(eng, "ld0", P.count("ld0"))
        P.wait(eng, "ldg", P.count("ldg"))
    LDG_ALL = P.count("ldg")

    for q in range(4):
        P.dma("sync", lambda e, q=q: e.dma_start(
            out=UB[0].ap()[:, q * 4096:(q + 1) * 4096],
            in_=bass.AP(u_in, q * 4096, [[16384, 128], [1, 4096]])), "uld")
    _steps0, _b1, _b2 = emit_prep1(0)
    PRE0.extend(_steps0)

    so("vector", lambda e: e.memset(ONES64, 1.0))
    ONE1 = onesrf.ap()[0:1, 0:1]

    so("scalar", lambda e: e.activation(ss("step"), ss("ls"), AF.Exp))
    so("vector", lambda e: e.reciprocal(ss("istep"), ss("step")))
    so("vector", lambda e: e.tensor_scalar(ss("c0"), ss("istep"), 2.0, None, ALU.mult))
    so("vector", lambda e: e.tensor_scalar(ss("c4"), ss("istep"), 4.0, None, ALU.mult))

    def vb(out, a, b, op):
        so("vector", lambda e: e.tensor_tensor(out, a, b, op))

    def vsc(out, a, s, op):
        so("vector", lambda e: e.tensor_scalar(out, a, s, None, op))

    T1, T2, T3, T4 = vs("t1"), vs("t2"), vs("t3"), vs("t4")

    def cmul(outr, outi, ar, ai, br, bi):
        vb(T1, ar, br, ALU.mult)
        vb(T2, ai, bi, ALU.mult)
        vb(T3, ar, bi, ALU.mult)
        vb(T4, ai, br, ALU.mult)
        vb(outr, T1, T2, ALU.subtract)
        vb(outi, T3, T4, ALU.add)

    D0r, D0i = vs("D0r"), vs("D0i")
    AR, DEN = vs("arr"), vs("den")
    vsc(AR, vs("lr"), -1.0, ALU.mult)
    so("vector", lambda e: e.tensor_scalar(AR, AR, ss("c0"), None, ALU.add))
    vb(DEN, AR, AR, ALU.mult)
    vb(T1, vs("li"), vs("li"), ALU.mult)
    vb(DEN, DEN, T1, ALU.add)
    so("vector", lambda e: e.reciprocal(vs("rden"), DEN))
    vb(D0r, AR, vs("rden"), ALU.mult)
    vb(D0i, vs("li"), vs("rden"), ALU.mult)

    DPr, DPi, QDr, QDi = vs("dpr"), vs("dpi"), vs("qdr"), vs("qdi")
    cmul(DPr, DPi, D0r, D0i, vs("pr"), vs("pi"))
    NQI = vs("nqi")
    vsc(NQI, vs("qi"), -1.0, ALU.mult)
    cmul(QDr, QDi, vs("qr"), NQI, D0r, D0i)
    SRv, SIv = vs("srv"), vs("siv")
    cmul(SRv, SIv, QDr, QDi, vs("pr"), vs("pi"))
    so("vector", lambda e: e.reduce_sum(ss("dAr"), SRv, axis=AX))
    so("vector", lambda e: e.reduce_sum(ss("dAi"), SIv, axis=AX))
    so("vector", lambda e: e.tensor_scalar(ss("dAr"), ss("dAr"), 1.0, None, ALU.add))
    vb(ss("dd2"), ss("dAr"), ss("dAr"), ALU.mult)
    vb(ss("tt"), ss("dAi"), ss("dAi"), ALU.mult)
    vb(ss("dd2"), ss("dd2"), ss("tt"), ALU.add)
    so("vector", lambda e: e.reciprocal(ss("rdd"), ss("dd2")))
    vb(ss("sgr"), ss("dAr"), ss("rdd"), ALU.mult)
    vb(ss("sgi"), ss("dAi"), ss("rdd"), ALU.mult)
    vsc(ss("sgi"), ss("sgi"), -1.0, ALU.mult)

    ORm = smat.ap()[0:64, 2 * 64:3 * 64]
    OIm = smat.ap()[0:64, 3 * 64:4 * 64]
    NDPi = vs("ndpi")
    vsc(NDPi, DPi, -1.0, ALU.mult)
    so("tensor", lambda e: e.matmul(psA.ap()[0:64, 0:64], DPr, QDr, start=True, stop=False))
    so("tensor", lambda e: e.matmul(psA.ap()[0:64, 0:64], NDPi, QDi, start=False, stop=True))
    so("tensor", lambda e: e.matmul(psA.ap()[0:64, 64:128], DPr, QDi, start=True, stop=False))
    so("tensor", lambda e: e.matmul(psA.ap()[0:64, 64:128], DPi, QDr, start=False, stop=True))
    so("vector", lambda e: e.tensor_copy(ORm, psA.ap()[0:64, 0:64]))
    so("vector", lambda e: e.tensor_copy(OIm, psA.ap()[0:64, 64:128]))
    SGRc = dcol.ap()[0:64, 0:1]
    SGIc = dcol.ap()[0:64, 1:2]
    so("tensor", lambda e: e.matmul(psX.ap()[0:64, 0:1], ONES64, ss("sgr"), start=True, stop=True))
    so("vector", lambda e: e.tensor_copy(SGRc, psX.ap()[0:64, 0:1]))
    so("tensor", lambda e: e.matmul(psX.ap()[0:64, 0:1], ONES64, ss("sgi"), start=True, stop=True))
    so("vector", lambda e: e.tensor_copy(SGIc, psX.ap()[0:64, 0:1]))
    SOr = smat.ap()[0:64, 4 * 64:5 * 64]
    SOi = smat.ap()[0:64, 5 * 64:6 * 64]
    TMm = smat.ap()[0:64, 6 * 64:7 * 64]
    so("vector", lambda e: e.tensor_scalar(SOr, ORm, SGRc, None, ALU.mult))
    so("vector", lambda e: e.tensor_scalar(TMm, OIm, SGIc, None, ALU.mult))
    vb(SOr, SOr, TMm, ALU.subtract)
    so("vector", lambda e: e.tensor_scalar(SOi, ORm, SGIc, None, ALU.mult))
    so("vector", lambda e: e.tensor_scalar(TMm, OIm, SGRc, None, ALU.mult))
    vb(SOi, SOi, TMm, ALU.add)
    DGr = smat.ap()[0:64, 7 * 64:8 * 64]
    DGi = smat.ap()[0:64, 8 * 64:9 * 64]
    so("tensor", lambda e: e.matmul(psX.ap()[0:64, 0:64], ONES64, D0r, start=True, stop=True))
    so("vector", lambda e: e.tensor_tensor(DGr, psX.ap()[0:64, 0:64], identf.ap()[0:64, 0:64], ALU.mult))
    so("tensor", lambda e: e.matmul(psX.ap()[0:64, 0:64], ONES64, D0i, start=True, stop=True))
    so("vector", lambda e: e.tensor_tensor(DGi, psX.ap()[0:64, 0:64], identf.ap()[0:64, 0:64], ALU.mult))
    A1r = smat.ap()[0:64, 9 * 64:10 * 64]
    A1i = smat.ap()[0:64, 10 * 64:11 * 64]
    NA1i = smat.ap()[0:64, 11 * 64:12 * 64]
    vb(A1r, DGr, SOr, ALU.subtract)
    vb(A1i, DGi, SOi, ALU.subtract)
    vsc(NA1i, A1i, -1.0, ALU.mult)

    RA1, RA1T = Xt[4], Xt[5]
    so("vector", lambda e: e.tensor_copy(RA1.ap()[0:64, 0:64], A1r))
    so("vector", lambda e: e.tensor_copy(RA1.ap()[0:64, 64:128], NA1i))
    so("vector", lambda e: e.tensor_copy(RA1.ap()[64:128, 0:64], A1i))
    so("vector", lambda e: e.tensor_copy(RA1.ap()[64:128, 64:128], A1r))
    so("tensor", lambda e: e.transpose(psA.ap()[:, 0:128], RA1.ap()[:, :], identf.ap()[:, :]))
    so("vector", lambda e: e.tensor_copy(RA1T.ap()[:, :], psA.ap()[:, 0:128]))

    C4c = dcol.ap()[:, 2:3]
    so("tensor", lambda e: e.matmul(psX.ap()[:, 0:1], onesrf.ap()[:, :], ss("c4"), start=True, stop=True))
    so("vector", lambda e: e.tensor_copy(C4c, psX.ap()[:, 0:1]))
    so("vector", lambda e: e.tensor_scalar(Sch[0].ap()[:, :], RA1.ap()[:, :], C4c, None, ALU.mult))
    vb(Sch[0].ap()[:, :], Sch[0].ap()[:, :], identf.ap()[:, :], ALU.subtract)
    so("vector", lambda e: e.tensor_scalar(Tch[0].ap()[:, :], RA1T.ap()[:, :], C4c, None, ALU.mult))
    vb(Tch[0].ap()[:, :], Tch[0].ap()[:, :], identf.ap()[:, :], ALU.subtract)

    BH = wtv.ap()[:, 1:2]
    so("tensor", lambda e: e.matmul(psX.ap()[0:64, 1:2], smat.ap()[0:64, 0:64], wtv.ap()[0:64, 2:3], start=True, stop=True))
    so("tensor", lambda e: e.matmul(psX.ap()[64:128, 1:2], smat.ap()[0:64, 64:128], wtv.ap()[0:64, 2:3], start=True, stop=True))
    so("vector", lambda e: e.tensor_copy(wtv.ap()[:, 3:4], psX.ap()[:, 1:2]))
    so("tensor", lambda e: e.matmul(psX.ap()[:, 1:2], RA1T.ap()[:, :], wtv.ap()[:, 3:4], start=True, stop=True))
    so("vector", lambda e: e.tensor_scalar(BH, psX.ap()[:, 1:2], 2.0, None, ALU.mult))

    for k in range(1, 13):
        so("tensor", lambda e, k=k: e.matmul(psA.ap()[:, 0:128], Tch[k - 1].ap()[:, :], Sch[k - 1].ap()[:, :], start=True, stop=True))
        so("vector", lambda e, k=k: e.tensor_copy(Sch[k].ap()[:, :], psA.ap()[:, 0:128]))
        so("tensor", lambda e, k=k: e.matmul(psA.ap()[:, 128:256], Sch[k - 1].ap()[:, :], Tch[k - 1].ap()[:, :], start=True, stop=True))
        so("vector", lambda e, k=k: e.tensor_copy(Tch[k].ap()[:, :], psA.ap()[:, 128:256]))

    vb(Xt[0].ap()[:, :], Sch[12].ap()[:, :], identf.ap()[:, :], ALU.add)
    vb(Xt[1].ap()[:, :], Tch[12].ap()[:, :], identf.ap()[:, :], ALU.add)
    for j in range(4):
        so("tensor", lambda e: e.matmul(psA.ap()[:, 0:128], Tch[12].ap()[:, :], Sch[12].ap()[:, :], start=True, stop=True))
        so("tensor", lambda e: e.matmul(psA.ap()[:, 128:256], Sch[12].ap()[:, :], Tch[12].ap()[:, :], start=True, stop=True))
        so("vector", lambda e: e.tensor_copy(Sch[12].ap()[:, :], psA.ap()[:, 0:128]))
        so("vector", lambda e: e.tensor_copy(Tch[12].ap()[:, :], psA.ap()[:, 128:256]))
        vb(Xt[2].ap()[:, :], Sch[12].ap()[:, :], identf.ap()[:, :], ALU.add)
        vb(Xt[3].ap()[:, :], Tch[12].ap()[:, :], identf.ap()[:, :], ALU.add)
        so("tensor", lambda e: e.matmul(psA.ap()[:, 0:128], Xt[1].ap()[:, :], Xt[2].ap()[:, :], start=True, stop=True))
        so("tensor", lambda e: e.matmul(psA.ap()[:, 128:256], Xt[2].ap()[:, :], Xt[1].ap()[:, :], start=True, stop=True))
        so("vector", lambda e: e.tensor_copy(Xt[0].ap()[:, :], psA.ap()[:, 0:128]))
        so("vector", lambda e: e.tensor_copy(Xt[1].ap()[:, :], psA.ap()[:, 128:256]))

    CH = wtv.ap()[:, 2:3]
    so("tensor", lambda e: e.matmul(psX.ap()[0:64, 2:3], vs("cr"), ONE1, start=True, stop=True))
    so("tensor", lambda e: e.matmul(psX.ap()[64:128, 2:3], vs("ci"), ONE1, start=True, stop=True))
    so("vector", lambda e: e.tensor_copy(CH, psX.ap()[:, 2:3]))
    WT = wtv.ap()[:, 0:1]
    so("tensor", lambda e: e.matmul(psX.ap()[:, 0:1], Xt[0].ap()[:, :], CH, start=True, stop=True))
    so("vector", lambda e: e.tensor_copy(WT, psX.ap()[:, 0:1]))

    so("vector", lambda e: e.tensor_copy(Vch.ap()[:, 0:1], BH))
    w, k = 1, 0
    while w < 512:
        so("tensor", lambda e, w=w, k=k: e.matmul(psA.ap()[:, 0:512][:, 0:w], Tch[k].ap()[:, :], Vch.ap()[:, 0:w], start=True, stop=True))
        so("vector", lambda e, w=w: e.tensor_copy(Vch.ap()[:, w:2 * w], psA.ap()[:, 0:512][:, 0:w]))
        w, k = w * 2, k + 1
    so("tensor", lambda e: e.matmul(psA.ap()[:, 0:128], Tch[9].ap()[:, :], Vch.ap()[:, 0:128], start=True, stop=True))
    so("vector", lambda e: e.tensor_copy(Vch.ap()[:, 512:640], psA.ap()[:, 0:128]))

    so("tensor", lambda e: e.matmul(psS.ap()[0:1, 0:512], WT, Vch.ap()[:, 0:512], start=True, stop=True))
    so("vector", lambda e: e.tensor_copy(krow.ap()[0:1, 0:512], psS.ap()[0:1, 0:512]))
    so("tensor", lambda e: e.matmul(psS.ap()[0:1, 0:128], WT, Vch.ap()[:, 512:640], start=True, stop=True))
    so("vector", lambda e: e.tensor_copy(krow.ap()[0:1, 512:640], psS.ap()[0:1, 0:128]))
    so("vector", lambda e: e.memset(kp_sb.ap()[0:1, :], 0.0))
    so("vector", lambda e: e.tensor_copy(kp_sb.ap()[0:1, 128:768], krow.ap()[0:1, 0:640]))
    sd(lambda e: e.dma_start(out=kp_dram.ap(), in_=kp_sb.ap()))

    so("tensor", lambda e: e.matmul(psX.ap()[:, 0:1], onesrf.ap()[:, :], ss("D"), start=True, stop=True))
    so("vector", lambda e: e.tensor_copy(dcol.ap()[:, 3:4], psX.ap()[:, 0:1]))
    for d in range(4):
        src = bass.AP(kp_dram, 1 + 128 * d, [[1, 128], [1, 128]])
        sd(lambda e, s=src: e.dma_start(out=Xt[2].ap()[:, :], in_=s))
        so("tensor", lambda e: e.matmul(psA.ap()[:, 0:128], jrev.ap()[:, :], Xt[2].ap()[:, :], start=True, stop=True))
        if d == 0:
            so("vector", lambda e: e.scalar_tensor_tensor(
                cm_T.ap()[:, 0:128], identf.ap()[:, :], dcol.ap()[:, 3:4],
                psA.ap()[:, 0:128], ALU.mult, ALU.add))
        else:
            so("vector", lambda e, d=d: e.tensor_copy(cm_T.ap()[:, 128 * d:128 * (d + 1)], psA.ap()[:, 0:128]))

    for jb in range(4):
        so("tensor", lambda e, jb=jb: e.transpose(psA.ap()[:, 0:128], Vch.ap()[:, 128 * (3 - jb):128 * (4 - jb)], identf.ap()[:, :]))
        so("vector", lambda e: e.tensor_copy(Xt[2].ap()[:, :], psA.ap()[:, 0:128]))
        so("tensor", lambda e: e.matmul(psA.ap()[:, 128:256], jrev.ap()[:, :], Xt[2].ap()[:, :], start=True, stop=True))
        so("vector", lambda e, jb=jb: e.tensor_copy(cm_ET.ap()[:, 128 * jb:128 * (jb + 1)], psA.ap()[:, 128:256]))

    so("tensor", lambda e: e.matmul(psA.ap()[:, 0:1], Sch[0].ap()[:, :], WT, start=True, stop=True))
    so("vector", lambda e: e.tensor_copy(Vch.ap()[:, 0:1], psA.ap()[:, 0:1]))
    w, k = 1, 0
    while w < 512:
        so("tensor", lambda e, w=w, k=k: e.matmul(psA.ap()[:, 0:512][:, 0:w], Sch[k].ap()[:, :], Vch.ap()[:, 0:w], start=True, stop=True))
        so("vector", lambda e, w=w: e.tensor_copy(Vch.ap()[:, w:2 * w], psA.ap()[:, 0:512][:, 0:w]))
        w, k = w * 2, k + 1
    so("vector", lambda e: e.tensor_copy(cm_O1T.ap()[:, :], Vch.ap()[:, 0:512]))
    so("vector", lambda e: e.tensor_copy(cm_MtT.ap()[:, :], Tch[9].ap()[:, :]))

    while PRE0:
        PRE0.pop(0)()
    emit_chain1(0, _b1, _b2)

    ONEC = state.ap()[:, 1020:1021]
    so("vector", lambda e: e.memset(ONEC, 1.0))
    STA = state.ap()[:, 1022:1023]
    STB = state.ap()[:, 1023:1024]
    P.wait("tensor", "seq")
    P.wait("scalar", "seq")
    for c in range(NCH):
        cur = STA if c % 2 == 0 else STB
        nxt = STB if c % 2 == 0 else STA
        if c > 0:
            P.wait("tensor", "c1s", c)
        for tb in range(NSUB):
            if c > 0:
                P.wait("tensor", "c1e", 4 * (c - 1) + tb + 1)
            first = True
            if c > 0:
                P.op("tensor", lambda e, tb=tb, cur=cur: e.matmul(
                    psA.ap()[:, tb * 512:tb * 512 + 1],
                    cm_O1T.ap()[:, 128 * tb:128 * (tb + 1)], cur, start=True, stop=False))
                first = False
            for jb in range(tb + 1):
                fst = first and jb == 0
                P.op("tensor", lambda e, tb=tb, jb=jb, fst=fst: e.matmul(
                    psA.ap()[:, tb * 512:tb * 512 + 1],
                    cm_T.ap()[:, 128 * (tb - jb):128 * (tb - jb + 1)], ONEC,
                    start=fst, stop=(jb == tb)), inc=("c1g" if jb == tb else None))
                first = False
            P.wait("scalar", "c1g", 4 * c + tb + 1)
            P.op("scalar", lambda e, c=c, tb=tb: e.copy(
                c1stage.ap()[:, 4 * c + tb:4 * c + tb + 1],
                psA.ap()[:, tb * 512:tb * 512 + 1]), inc="c1e")
        for jb in range(NSUB):
            last0 = (c == 0 and jb == NSUB - 1)
            P.op("tensor", lambda e, jb=jb, last0=last0: e.matmul(
                psS.ap()[:, 0:1], cm_ET.ap()[:, 128 * jb:128 * (jb + 1)], ONEC,
                start=(jb == 0), stop=last0), inc=("c1t" if last0 else None))
        if c > 0:
            P.op("tensor", lambda e, cur=cur: e.matmul(
                psS.ap()[:, 0:1], cm_MtT.ap()[:, :], cur, start=False, stop=True), inc="c1t")
        P.wait("scalar", "c1t", c + 1)
        P.op("scalar", lambda e, nxt=nxt: e.copy(nxt, psS.ap()[:, 0:1]), inc="c1s")
    P.wait("tensor", "c1e", 32)
    P.wait("tensor", "c1s", 8)
    so("tensor", lambda e: e.transpose(psA.ap()[:, 0:128][0:32, :], c1stage.ap()[:, :], identf.ap()[:, :]))
    so("vector", lambda e: e.tensor_copy(Xt[2].ap()[0:32, :], psA.ap()[:, 0:128][0:32, :]))
    dstc1 = bass.AP(c1row, 0, [[DM, 1], [128, 32], [1, 128]])
    srcc1 = bass.AP(Xt[2], 0, [[128, 32], [1, 128]])
    gd(lambda e: e.dma_start(out=dstc1, in_=srcc1))

    PH0_DONE = P.count("seq")
    PH0_DMAD = P.count("sqd")
    PH0_DMAG = P.count("sqg")

    for s in range(1, NS1):
        if s == 1:
            P.wait("sync", "seq", PH0_DONE)
            P.wait("sync", "sqd", PH0_DMAD)
            P.wait("sync", "sqg", PH0_DMAG)
        for q in range(4):
            if s >= 2:
                P.wait("sync", "spf", 64 * (s - 2) + 16 * (q + 1))
                P.wait("sync", "spb", 64 * (s - 2) + 16 * (q + 1))
            P.dma("sync", lambda e, s=s, q=q, buf=UB[s % 2]: e.dma_start(
                out=buf.ap()[:, q * 4096:(q + 1) * 4096],
                in_=bass.AP(u_in, s * 128 * 16384 + q * 4096, [[16384, 128], [1, 4096]])), "uld")



    for s in range(NS1):
        buf = UB[s % 2]
        for eng in ("tensor", "vector", "scalar"):
            P.wait(eng, "uld", uld_full(s))
            if s == 0:
                P.wait(eng, "seq", PH0_DONE)
                P.wait(eng, "sqd", PH0_DMAD)
                P.wait(eng, "sqg", PH0_DMAG)
        nxt_steps = []
        if s + 1 < NS1:
            nxt_steps, nb1, nb2 = emit_prep1(s + 1)

        def emit_ln1(ls, c):
            lbuf = UB[ls % 2]
            G = ls * NCH + c
            for j in range(4):
                b = 4 * c + j
                wslot = (c % 2) * 4 + j
                P.wait("scalar", "uld", uld_q(ls, b))
                if G >= 2:
                    P.wait("scalar", "t_cv", tcv_after[G - 2])
                P.op("scalar", lambda e, b=b, w=wslot, lbuf=lbuf: e.copy(
                    LN1.ap()[:, w * 512:(w + 1) * 512],
                    blk_ap(lbuf, b)), inc="v_ln")

        sl = s % 2
        if s == 0:
            vln_base[0] = P.count("v_ln")
            emit_ln1(0, 0)
        vxe_b = P.count("v_xe")
        for c in range(NCH):
            if c + 1 < NCH and (s == 0 or c > 0):
                emit_ln1(s, c + 1)
            cur = state.ap()[:, (c % 2) * 512:(c % 2) * 512 + 512]
            nxt = state.ap()[:, ((c + 1) % 2) * 512:((c + 1) % 2) * 512 + 512]
            P.wait("tensor", "v_ln", vln_base[s] + 4 * (c + 1))
            P.wait("tensor", "sx", rb_cnt[s])
            P.wait("tensor", "s_st")
            for jb in range(NSUB):
                wslot = (c % 2) * 4 + jb
                last0 = (c == 0 and jb == NSUB - 1)
                P.op("tensor", lambda e, jb=jb, w=wslot, last0=last0: e.matmul(
                    psS.ap()[:, :], cm_ET.ap()[:, 128 * jb:128 * (jb + 1)],
                    LN1.ap()[:, w * 512:(w + 1) * 512],
                    start=(jb == 0), stop=last0), inc=("t_sc" if last0 else None))
            if c > 0:
                P.op("tensor", lambda e, cur=cur: e.matmul(
                    psS.ap()[:, :], cm_MtT.ap()[:, :], cur, start=False, stop=True), inc="t_sc")
            P.wait("vector", "t_sc")
            P.op("vector", lambda e, nxt=nxt: e.tensor_copy(nxt, psS.ap()[:, :]), inc="s_st")
            for tb in range(NSUB):
                gg = P.count("t_cv")
                bank = gg % 4
                if gg >= 4:
                    P.wait("tensor", "v_tm", gg - 3)
                first = True
                if c > 0:
                    P.op("tensor", lambda e, tb=tb, cur=cur, bank=bank: e.matmul(
                        psA.ap()[:, bank * 512:(bank + 1) * 512],
                        cm_O1T.ap()[:, 128 * tb:128 * (tb + 1)], cur, start=True, stop=False))
                    first = False
                for jb in range(tb + 1):
                    fst = first and jb == 0
                    wslot = (c % 2) * 4 + jb
                    P.op("tensor", lambda e, tb=tb, w=wslot, d=tb - jb, fst=fst, bank=bank: e.matmul(
                        psA.ap()[:, bank * 512:(bank + 1) * 512],
                        cm_T.ap()[:, 128 * d:128 * (d + 1)],
                        LN1.ap()[:, w * 512:(w + 1) * 512], start=fst, stop=False))
                    first = False
                tblk = 4 * c + tb
                P.op("tensor", lambda e, tblk=tblk, bank=bank, sl=sl: e.matmul(
                    psA.ap()[:, bank * 512:(bank + 1) * 512],
                    c1row.ap()[0:1, 128 * tblk:128 * (tblk + 1)],
                    negmr.ap()[0:1, sl * 512:(sl + 1) * 512], start=False, stop=True), inc="t_cv")
                b = 4 * c + tb
                P.wait("vector", "t_cv", P.count("t_cv"))
                P.wait("vector", "sx", rb_cnt[s])
                if gg >= 2:
                    P.wait("vector", "v_xe", P.count("v_xe") - 1)
                q2 = gg % 2
                P.op("vector", lambda e, bank=bank, sl=sl, q2=q2: e.tensor_mul(
                    cvt.ap()[:, q2 * 512:(q2 + 1) * 512],
                    psA.ap()[:, bank * 512:(bank + 1) * 512],
                    Rb.ap()[:, sl * 512:(sl + 1) * 512]), inc="v_tm")
                P.wait("vector", "v_tm")
                P.op("vector", lambda e, b=b, buf=buf, q2=q2: e.tensor_add(
                    blk_ap(buf, b), blk_ap(buf, b),
                    cvt.ap()[:, q2 * 512:(q2 + 1) * 512]), inc="v_xe")
            tcv_after.append(P.count("t_cv"))
            if c % 2 == 1:
                q = c // 2
                P.wait("scalar", "v_xe", vxe_b + 8 * (q + 1))
                P.dma("scalar", lambda e, s=s, buf=buf, q=q: e.dma_start(
                    out=bass.AP(xsp_f, s * 128 * 16384 + q * 4096, [[16384, 128], [1, 4096]]),
                    in_=buf.ap()[:, q * 4096:(q + 1) * 4096]), "spf")
                P.wait("gpsimd", "v_xe", vxe_b + 8 * (q + 1))
                P.dma("gpsimd", lambda e, s=s, buf=buf, q=q: e.dma_start(
                    out=bass.AP(xsp_b, s * 128 * 16384 + q * 4096, [[16384, 128], [1, 4096]]),
                    in_=buf.ap()[:, q * 4096:(q + 1) * 4096]), "spb")
            if 3 <= c <= 6:
                for _ in range(2):
                    if nxt_steps:
                        nxt_steps.pop(0)()
            if c == 6:
                while nxt_steps:
                    nxt_steps.pop(0)()
                if s + 1 < NS1:
                    vln_base[s + 1] = P.count("v_ln")
                    emit_ln1(s + 1, 0)
                    emit_chain1(s + 1, nb1, nb2)
            if c == 7 and s + 1 < NS1:
                emit_ln1(s + 1, 1)
            if s == 3 and c == 0:
                P.wait("gpsimd", "spf", 64 * 3)
                P.wait("gpsimd", "spb", 64 * 3)
                for db in range(16):
                    P.dma("gpsimd", lambda e, db=db: e.dma_start(
                        out=At_sb.ap()[:, db * 1024:(db + 1) * 1024],
                        in_=at_in.ap()[128 * db:128 * (db + 1), :]), "at1", serialize=False)
                for db in range(16, 32):
                    P.dma("gpsimd", lambda e, db=db: e.dma_start(
                        out=At_sb.ap()[:, db * 1024:(db + 1) * 1024],
                        in_=at_in.ap()[128 * db:128 * (db + 1), :]), "at2", serialize=False)

    P.wait("gpsimd", "t_h", 24)
    for kb in range(4):
        P.dma("gpsimd", lambda e, kb=kb: e.dma_start(
            out=Bt_lo.ap()[:, kb * DM:(kb + 1) * DM],
            in_=bt_in.ap()[128 * kb:128 * (kb + 1), :]), "btl", serialize=False)

    for s in range(NS1):
        P.wait("sync", "spb", 64 * (s + 1))
        if s == 0:
            P.wait("sync", "spf", 208)
            P.wait("sync", "spb", 208)
            P.dma("sync", lambda e: e.dma_start(
                out=X2[0].ap()[:, 0:8192],
                in_=bass.AP(xsp_b, 0, [[NBLK * F1, 128], [1, 8192]])), "x2h")
            P.wait("sync", "spf", 224)
            P.wait("sync", "spb", 224)
            P.wait("sync", "x2h", 16)
            P.dma("sync", lambda e: e.dma_start(
                out=X2[0].ap()[:, 8192:16384],
                in_=bass.AP(xsp_b, 8192, [[NBLK * F1, 128], [1, 8192]])), "x2l")
            continue
        P.wait("sync", "spf", 256)
        P.wait("sync", "spb", 256)
        if s >= 2:
            P.wait("sync", "t_h", 8 * (s - 1))
        P.dma("sync", lambda e, s=s: e.dma_start(out=X2[s % 2].ap()[:, :], in_=xsp_b.ap()[s]), "x2l")


    def emit_prep2(s):
        xb = X2[s % 2]
        base_sq = P.count("t_sq")
        base_vq = P.count("v_sq")
        sx_prev = P.count("sx")
        steps = []
        for g8 in range(8):
            def step(g8=g8, xb=xb):
                for b in range(4 * g8, 4 * g8 + 4):
                    q = b % 2
                    if s == 0 and b < 16:
                        P.wait("vector", "x2h", 16)
                    else:
                        P.wait("vector", "x2l", 16 * (s + 1))
                    if b == 0:
                        P.wait("vector", "sx", sx_prev)
                    if b >= 2:
                        P.wait("vector", "t_sq", base_sq + (b - 1))
                    P.op("vector", lambda e, b=b, q=q, xb=xb: e.tensor_mul(
                        sqs.ap()[:, q * 512:(q + 1) * 512], blk_ap(xb, b), blk_ap(xb, b)), inc="v_sq")
                    P.wait("tensor", "v_sq", base_vq + b + 1)
                    if b == 0:
                        P.wait("tensor", "sx", sx_prev)
                        P.wait("tensor", "v_a1", 3)
                    P.op("tensor", lambda e, b=b, q=q: e.matmul(
                        ps2.ap()[:, :], sumw.ap()[:, :], sqs.ap()[:, q * 512:(q + 1) * 512],
                        start=(b == 0), stop=(b == NBLK - 1), skip_group_check=True), inc="t_sq")
            steps.append(step)
        return steps, base_sq

    def emit_chain2(s, base_sq):
        sl = s % 2
        P.wait("vector", "t_sco")
        P.wait("vector", "t_sq", base_sq + NBLK)
        if s >= 2:
            P.wait("vector", "t_h", 8 * (s - 1))
        chain("vector", lambda e: e.tensor_scalar(M_AP, psX.ap()[64:65, :], 1.0 / DM, None, ALU.mult))
        chain("vector", lambda e: e.tensor_scalar(V_AP, ps2.ap()[0:1, :], 1.0 / DM, None, ALU.mult))
        chain("vector", lambda e: e.tensor_mul(R_AP, M_AP, M_AP))
        chain("vector", lambda e: e.tensor_sub(V_AP, V_AP, R_AP))
        chain("vector", lambda e: e.tensor_scalar(V_AP, V_AP, EPS, None, ALU.add))
        chain("scalar", lambda e: e.activation(R_AP, V_AP, AF.Ln))
        chain("scalar", lambda e: e.activation(R_AP, R_AP, AF.Exp, scale=-0.5))
        chain("vector", lambda e, sl=sl: e.tensor_scalar(nm2.ap()[0:1, sl * 512:(sl + 1) * 512], M_AP, -1.0, None, ALU.mult))
        for t in range(4):
            chain("tensor", lambda e, t=t: e.matmul(
                psA.ap()[:, 1536 + t:1537 + t], R_AP[0:1, 128 * t:128 * (t + 1)], ONE1, start=True, stop=True))
        chain("vector", lambda e, sl=sl: e.tensor_copy(r2c.ap()[:, sl * 4:(sl + 1) * 4], psA.ap()[:, 1536:1540]))
        return P.count("sx")

    def emit_score_fix(s, sx_r2v):
        sl = s % 2
        P.wait("tensor", "sx", sx_r2v)
        P.op("tensor", lambda e, sl=sl: e.matmul(
            psX.ap()[0:64, :], W1row.ap()[0:1, :], nm2.ap()[0:1, sl * 512:(sl + 1) * 512],
            start=False, stop=True, skip_group_check=True), inc="t_sco")
        P.wait("scalar", "t_sco")
        P.op("scalar", lambda e: e.copy(sc_sb.ap()[:, :], psX.ap()[0:64, :]), inc="s_sco")

    def emit_scores(s):
        xb = X2[s % 2]
        if s == 0:
            P.wait("tensor", "x2h", 16)
            P.wait("tensor", "v_a1", 1)
        else:
            P.wait("tensor", "x2l", 16 * (s + 1))
        P.wait("tensor", "sx")
        P.wait("tensor", "s_sco")
        for db in range(NBLK):
            if s == 0 and db == 16:
                P.wait("tensor", "x2l", 16)
            P.op("tensor", lambda e, db=db, xb=xb: e.matmul(
                psX.ap()[0:65, :], wrt_b.ap()[:, db * 66:db * 66 + 65], blk_ap(xb, db),
                start=(db == 0), stop=(db == NBLK - 1), skip_group_check=True),
                inc=("t_sco" if db == NBLK - 1 else None))

    def gch(eng, fn):
        P.wait(eng, "gq")
        P.op(eng, fn, inc="gq")

    def emit_gating_pre(s):
        P.wait("tensor", "s_sco")
        for t in range(4):
            gch("tensor", lambda e, t=t: e.transpose(
                psA.ap()[:, 1536 + 64 * t:1600 + 64 * t],
                sc_sb.ap()[0:64, 128 * t:128 * (t + 1)], identf.ap()[0:64, 0:64]))

    def emit_gating_chain(s, t):
        sl = s % 2
        gch("vector", lambda e, t=t, sl=sl: e.tensor_scalar(
            sT_sb.ap()[:, :], psA.ap()[:, 1536 + 64 * t:1600 + 64 * t],
            r2c.ap()[:, sl * 4 + t:sl * 4 + t + 1], None, ALU.mult))
        gch("vector", lambda e: e.max(m8a.ap()[:, :], sT_sb.ap()[:, :]))
        gch("vector", lambda e: e.match_replace(gt_f.ap()[:, :], m8a.ap()[:, :], sT_sb.ap()[:, :], -1e30))
        gch("vector", lambda e: e.max(m8b.ap()[:, :], gt_f.ap()[:, :]))
        gch("scalar", lambda e: e.mul(z_sb.ap()[:, 2:3], m8a.ap()[:, 0:1], -1.0))
        gch("scalar", lambda e: e.activation(e_sb.ap()[:, :], sT_sb.ap()[:, :], AF.Exp, bias=z_sb.ap()[:, 2:3], scale=1.0))
        gch("vector", lambda e: e.tensor_scalar(mask_sb.ap()[:, :], sT_sb.ap()[:, :], m8b.ap()[:, 7:8], None, ALU.is_ge))
        gch("vector", lambda e: e.tensor_mul(e_sb.ap()[:, :], e_sb.ap()[:, :], mask_sb.ap()[:, :]))
        gch("vector", lambda e: e.reduce_sum(z_sb.ap()[:, 0:1], e_sb.ap()[:, :], axis=AX))
        gch("vector", lambda e: e.reciprocal(z_sb.ap()[:, 1:2], z_sb.ap()[:, 0:1]))
        gch("vector", lambda e, t=t, sl=sl: e.tensor_scalar(
            gtf4.ap()[:, 64 * t:64 * (t + 1)], e_sb.ap()[:, :], z_sb.ap()[:, 1:2],
            r2c.ap()[:, sl * 4 + t:sl * 4 + t + 1], ALU.mult, ALU.mult))

    def emit_gating_post(s):
        sl = s % 2
        for t in range(4):
            q = t % 2
            gch("tensor", lambda e, t=t, q=q: e.transpose(
                psA.ap()[:, 1792 + 128 * q:1920 + 128 * q][0:64, :],
                gtf4.ap()[:, 64 * t:64 * (t + 1)], identf.ap()[:, :]))
            gch("scalar", lambda e, t=t, q=q, sl=sl: e.copy(
                ghx.ap()[64:128, sl * 512 + 128 * t:sl * 512 + 128 * (t + 1)],
                psA.ap()[:, 1792 + 128 * q:1920 + 128 * q][0:64, :]))

    P.wait("tensor", "ldg", LDG_ALL)
    P.wait("tensor", "sx")
    for db in range(NBLK):
        P.op("tensor", lambda e, db=db: e.matmul(
            ps2.ap()[:, 0:64], sumw.ap()[:, :],
            wrt_b.ap()[:, db * 66:db * 66 + 64], start=(db == 0), stop=(db == NBLK - 1),
            skip_group_check=True))
    P.wait("tensor", "at1", 16 * 16)
    for db in range(16):
        P.op("tensor", lambda e, db=db: e.matmul(
            psX.ap()[0:128, :], sumw.ap()[:, :],
            At_sb.ap()[:, db * 1024:db * 1024 + 512], start=(db == 0), stop=False,
            skip_group_check=True))
        P.op("tensor", lambda e, db=db: e.matmul(
            ps1.ap()[:, :], sumw.ap()[:, :],
            At_sb.ap()[:, db * 1024 + 512:db * 1024 + 1024], start=(db == 0), stop=False,
            skip_group_check=True))
    P.wait("tensor", "at2", 16 * 16)
    for db in range(16, NBLK):
        P.op("tensor", lambda e, db=db: e.matmul(
            psX.ap()[0:128, :], sumw.ap()[:, :],
            At_sb.ap()[:, db * 1024:db * 1024 + 512], start=False, stop=(db == NBLK - 1),
            skip_group_check=True))
        P.op("tensor", lambda e, db=db: e.matmul(
            ps1.ap()[:, :], sumw.ap()[:, :],
            At_sb.ap()[:, db * 1024 + 512:db * 1024 + 1024], start=False, stop=(db == NBLK - 1),
            skip_group_check=True),
            inc=("t_a1" if db == NBLK - 1 else None))
    P.wait("vector", "t_a1")
    P.op("vector", lambda e: e.tensor_copy(A1row.ap()[0:1, 0:512], psX.ap()[0:1, :]), inc="v_a1")
    P.op("vector", lambda e: e.tensor_copy(A1row.ap()[0:1, 512:1024], ps1.ap()[0:1, :]), inc="v_a1")
    P.op("vector", lambda e: e.tensor_copy(W1row.ap()[0:1, :], ps2.ap()[0:1, 0:64]), inc="v_a1")

    emit_scores(0)
    st0, c2_ = emit_prep2(0)
    for st in st0:
        st()
    sx_r2 = emit_chain2(0, c2_)
    emit_score_fix(0, sx_r2)


    htl_after = []
    gq_done = {}
    emit_gating_pre(0)
    for t in range(4):
        emit_gating_chain(0, t)
    emit_gating_post(0)
    gq_done[0] = P.count("gq")

    for s in range(NS1):
        sl = s % 2
        xb = X2[s % 2]
        for eng in ("tensor", "vector", "scalar"):
            P.wait(eng, "x2l", 16 * (s + 1))
        nxt_steps = []
        if s + 1 < NS1:
            nxt_steps, nb2 = emit_prep2(s + 1)

        for kr in range(8):
            pb = kr % 2
            if s * 8 + kr >= 2:
                P.wait("tensor", "v_ht", P.count("v_ht") - 1)
            for db in range(NBLK):
                P.op("tensor", lambda e, kr=kr, db=db, pb=pb, xb=xb: e.matmul(
                    psA.ap()[:, pb * 512:(pb + 1) * 512],
                    At_sb.ap()[:, db * 1024 + kr * 128:db * 1024 + (kr + 1) * 128],
                    blk_ap(xb, db), start=(db == 0), stop=False))
            P.op("tensor", lambda e, kr=kr, pb=pb, sl=sl: e.matmul(
                psA.ap()[:, pb * 512:(pb + 1) * 512],
                A1row.ap()[0:1, kr * 128:(kr + 1) * 128],
                nm2.ap()[0:1, sl * 512:(sl + 1) * 512],
                start=False, stop=True), inc="t_h")
            gslot = kr % 2
            gps = psA.ap()[:, 1024:1536] if gslot == 0 else psS.ap()[:, :]
            P.wait("tensor", "gq", gq_done[s])
            if s * 8 + kr >= 2:
                P.wait("tensor", "s_gp", P.count("s_gp") - 1)
            P.op("tensor", lambda e, kr=kr, sl=sl, gps=gps: e.matmul(
                gps, rghx.ap()[64:128, kr * 128:(kr + 1) * 128],
                ghx.ap()[64:128, sl * 512:(sl + 1) * 512], start=True, stop=True), inc="t_gr")
            P.wait("scalar", "t_gr")
            if s * 8 + kr >= 2:
                P.wait("scalar", "v_ht", P.count("v_ht") - 1)
            P.op("scalar", lambda e, gslot=gslot, gps=gps: e.copy(
                grep.ap()[:, gslot * 512:(gslot + 1) * 512], gps), inc="s_gp")
            P.wait("vector", "t_h")
            P.wait("vector", "s_gp")
            P.op("vector", lambda e, kr=kr, s=s, pb=pb, gslot=gslot: e.tensor_mul(
                htl_ap(kr, s * 512, s * 512 + 512),
                psA.ap()[:, pb * 512:(pb + 1) * 512],
                grep.ap()[:, gslot * 512:(gslot + 1) * 512]), inc="v_ht")
            if s + 1 < NS1:
                if kr <= 3:
                    for _ in range(2):
                        if nxt_steps:
                            nxt_steps.pop(0)()
                elif kr == 4:
                    while nxt_steps:
                        nxt_steps.pop(0)()
                    emit_scores(s + 1)
                elif kr == 5:
                    sx_r2 = emit_chain2(s + 1, nb2)
                    emit_score_fix(s + 1, sx_r2)
                    emit_gating_pre(s + 1)
                elif kr == 6:
                    emit_gating_chain(s + 1, 0)
                    emit_gating_chain(s + 1, 1)
                elif kr == 7:
                    emit_gating_chain(s + 1, 2)
                    emit_gating_chain(s + 1, 3)
                    emit_gating_post(s + 1)
                    gq_done[s + 1] = P.count("gq")
        htl_after.append(P.count("v_ht"))

    P.wait("gpsimd", "t_h", 32)
    for kb in range(4, 8):
        P.dma("gpsimd", lambda e, kb=kb: e.dma_start(
            out=Bt_hi.ap()[:, (kb - 4) * DM:(kb - 3) * DM],
            in_=bt_in.ap()[128 * kb:128 * (kb + 1), :]), "bhi")
    for s2 in range(NS3):
        s1, g = s2 // 2, s2 % 2
        srcap = bass.AP(xsp_f, s1 * 128 * (NBLK * F1) + g * 8192, [[NBLK * F1, 128], [1, 8192]])
        P.wait("sync", "spf", 64 * (s1 + 1))
        P.wait("sync", "x2l", 16 * 4)
        P.wait("sync", "t_h", 8 * 4)
        if s2 >= 2:
            P.wait("sync", "yd", 64 * (s2 - 1))
        for q in range(4):
            qsrc = bass.AP(xsp_f, s1 * 128 * (NBLK * F1) + g * 8192 + q * 2048,
                           [[NBLK * F1, 128], [1, 2048]])
            P.dma("sync", lambda e, s2=s2, q=q, qsrc=qsrc: e.dma_start(
                out=XF[s2 % 2].ap()[:, q * 2048:(q + 1) * 2048], in_=qsrc), "xfl")

    for s2 in range(NS3):
        s1, g = s2 // 2, s2 % 2
        xf = XF[s2 % 2]
        for dd in range(16):
            d = 16 * g + dd
            pb = dd % 2
            if s2 == 0 and dd < 2:
                P.wait("tensor", "v_ht", htl_after[-1] - 1 + dd)
            if s2 * 16 + dd >= 2:
                P.wait("tensor", "v_y", P.count("v_y") - 1)
            for kb in range(8):
                bt = Bt_lo if kb < 4 else Bt_hi
                kbo = kb % 4
                if s2 == 0 and dd == 0:
                    if kb == 0:
                        P.wait("tensor", "btl", 16 * 4)
                    else:
                        P.wait("tensor", "bhi", 16 * (kb - 4 + 1)) if kb >= 4 else None
                P.op("tensor", lambda e, d=d, kbo=kbo, kb=kb, pb=pb, s1=s1, bt=bt: e.matmul(
                    psA.ap()[:, pb * 512:(pb + 1) * 512],
                    bt.ap()[:, kbo * DM + d * 128:kbo * DM + (d + 1) * 128],
                    htl_ap(kb, s1 * 512, (s1 + 1) * 512),
                    start=(kb == 0), stop=(kb == 7)), inc=("t_o" if kb == 7 else None))
            P.wait("vector", "t_o")
            P.wait("vector", "xfl", 64 * s2 + 16 * (dd // 4 + 1))
            P.op("vector", lambda e, dd=dd, pb=pb, xf=xf: e.scalar_tensor_tensor(
                xf.ap()[:, dd * 512:(dd + 1) * 512], psA.ap()[:, pb * 512:(pb + 1) * 512],
                2.0, xf.ap()[:, dd * 512:(dd + 1) * 512], ALU.mult, ALU.add), inc="v_y")
        for q in range(4):
            dstap = bass.AP(y_out, s1 * 128 * (NBLK * F1) + g * 8192 + q * 2048,
                            [[NBLK * F1, 128], [1, 2048]])
            P.wait("scalar", "v_y", 16 * s2 + 4 * (q + 1))
            P.dma("scalar", lambda e, dstap=dstap, xf=xf, q=q: e.dma_start(
                out=dstap, in_=xf.ap()[:, q * 2048:(q + 1) * 2048]), "yd")

    P.wait("scalar", "yd", 64 * NS3)
    P.wait("sync", "yd", 64 * NS3)

    with nc.Block() as blk:
        P.emit_into(blk)
    return nc


_NC_CACHE = None


def _get_nc():
    global _NC_CACHE
    if _NC_CACHE is None:
        _NC_CACHE = build_nc()
    return _NC_CACHE


def kernel(L0_attn_u, L0_attn_Lambda_ri, L0_attn_p_ri, L0_attn_q_ri,
           L0_attn_Vc_ri, L0_attn_Ct_ri, L0_attn_Bp, L0_attn_D,
           L0_attn_log_step, L0_ffn_A, L0_ffn_B, L0_ffn_Wr):
    nc = _get_nc()
    urows = np.ascontiguousarray(np.asarray(L0_attn_u, dtype=np.float32).reshape(16384, DM))

    At = np.ascontiguousarray(np.asarray(L0_ffn_A, np.float32).transpose(2, 0, 1).reshape(DM, 1024))
    Bt = np.ascontiguousarray(np.asarray(L0_ffn_B, np.float32).transpose(0, 2, 1).reshape(1024, DM))
    Wrt0 = np.asarray(L0_ffn_Wr, np.float32).T
    Wrt = np.zeros((DM, 66), np.float32)
    Wrt[:, 0:64] = Wrt0
    Wrt[:, 64] = 1.0

    shared = {
        "at_in": At, "bt_in": Bt, "wrt_in": Wrt,
        "lam2": np.ascontiguousarray(np.asarray(L0_attn_Lambda_ri, np.float32).T),
        "p2": np.ascontiguousarray(np.asarray(L0_attn_p_ri, np.float32).T),
        "q2": np.ascontiguousarray(np.asarray(L0_attn_q_ri, np.float32).T),
        "ct2": np.ascontiguousarray(np.asarray(L0_attn_Ct_ri, np.float32)[0].T),
        "vctr": np.ascontiguousarray(np.asarray(L0_attn_Vc_ri, np.float32)[:, :, 0].T),
        "vcti": np.ascontiguousarray(np.asarray(L0_attn_Vc_ri, np.float32)[:, :, 1].T),
        "bp_in": np.asarray(L0_attn_Bp, np.float32).reshape(64, 1),
        "dd_in": np.asarray(L0_attn_D, np.float32).reshape(1, 1),
        "ls_in": np.asarray(L0_attn_log_step, np.float32).reshape(1, 1),
        "identf_in": np.eye(128, dtype=np.float32),
        "identb_in": np.eye(128).astype(BF),
        "jrev_in": np.ascontiguousarray(np.eye(128, dtype=np.float32)[::-1]),
        "onesr_in": np.ones((1, 128), BF),
        "onesrf_in": np.ones((1, 128), np.float32),
        "onesc_in": np.ones((128, 1), BF),
        "sumw_in": (np.arange(128)[None, :] == 0).astype(BF) * np.ones((128, 1), BF),
        "repl_in": (np.arange(64)[:, None] == (np.arange(1024)[None, :] // 16)).astype(BF),
    }

    in_maps = []
    for cidx in range(8):
        rows = urows[cidx * NTOK:(cidx + 1) * NTOK]
        ut = rows.T.reshape(NBLK, 128, NS1, F1).transpose(2, 1, 0, 3).reshape(NS1, 128, NBLK * F1)
        d = dict(shared)
        d["u_in"] = np.ascontiguousarray(ut)
        in_maps.append(d)

    res = run_bass_kernel_spmd(nc, in_maps, core_ids=list(range(8)))

    out = np.empty((16384, DM), np.float32)
    for cidx in range(8):
        yt = res.results[cidx]["y_out"]
        yt = yt.reshape(NS1, 128, NBLK, F1).transpose(2, 1, 0, 3).reshape(DM, NTOK)
        out[cidx * NTOK:(cidx + 1) * NTOK] = yt.T
    return out.reshape(2, 8192, DM)


# version 41
# speedup vs baseline: 1.0294x; 1.0294x over previous
import numpy as np
import ml_dtypes
import concourse.bass as bass
import concourse.mybir as mybir
from concourse.bass_utils import run_bass_kernel_spmd

F32 = mybir.dt.float32
BF16 = mybir.dt.bfloat16
AF = mybir.ActivationFunctionType
ALU = mybir.AluOpType
AX = mybir.AxisListType.X

NTOK = 2048
DM = 4096
NBLK = 32
F1 = 512
NS1 = 4
NS3 = 8
F3 = 256
C = 512
NCH = 8
NSUB = 4
EPS = 1e-5
BF = ml_dtypes.bfloat16


class Plan:
    ENGINES = ("tensor", "vector", "scalar", "gpsimd", "sync")

    def __init__(self, nc):
        self.nc = nc
        self.ops = {e: [] for e in self.ENGINES}
        self.sems = {}
        self.counts = {}

    def sem(self, name):
        if name not in self.sems:
            self.sems[name] = self.nc.alloc_semaphore(name)
            self.counts[name] = 0
        return self.sems[name]

    def wait(self, eng, name, val=None):
        self.sem(name)
        v = self.counts[name] if val is None else val
        if v <= 0:
            return
        h = self.sems[name]
        self.ops[eng].append(lambda e, h=h, v=v: e.wait_ge(h, v))

    def op(self, eng, fn, inc=None, amt=1):
        if inc is not None:
            h = self.sem(inc)
            self.counts[inc] += amt
            self.ops[eng].append(lambda e, fn=fn, h=h, a=amt: fn(e).then_inc(h, a))
        else:
            self.ops[eng].append(lambda e, fn=fn: fn(e))

    def dma(self, eng, fn, inc, serialize=True):
        if serialize and self.count(inc) > 0:
            self.wait(eng, inc, self.count(inc))
        self.op(eng, fn, inc=inc, amt=16)

    def count(self, name):
        self.sem(name)
        return self.counts[name]

    def emit_into(self, block):
        for eng in self.ENGINES:
            lst = self.ops[eng]
            if not lst:
                continue

            def runner(e, lst=lst):
                for fn in lst:
                    fn(e)
            getattr(block, eng)(runner)


def build_nc():
    nc = bass.Bass("TRN2", target_bir_lowering=False)

    u_in = nc.dram_tensor("u_in", [NS1, 128, NBLK * F1], F32, kind="ExternalInput")
    at_in = nc.dram_tensor("at_in", [DM, 1024], F32, kind="ExternalInput")
    bt_in = nc.dram_tensor("bt_in", [1024, DM], F32, kind="ExternalInput")
    wrt_in = nc.dram_tensor("wrt_in", [DM, 66], F32, kind="ExternalInput")
    lam2 = nc.dram_tensor("lam2", [2, 64], F32, kind="ExternalInput")
    p2 = nc.dram_tensor("p2", [2, 64], F32, kind="ExternalInput")
    q2 = nc.dram_tensor("q2", [2, 64], F32, kind="ExternalInput")
    ct2 = nc.dram_tensor("ct2", [2, 64], F32, kind="ExternalInput")
    vctr = nc.dram_tensor("vctr", [64, 64], F32, kind="ExternalInput")
    vcti = nc.dram_tensor("vcti", [64, 64], F32, kind="ExternalInput")
    bp_in = nc.dram_tensor("bp_in", [64, 1], F32, kind="ExternalInput")
    dd_in = nc.dram_tensor("dd_in", [1, 1], F32, kind="ExternalInput")
    ls_in = nc.dram_tensor("ls_in", [1, 1], F32, kind="ExternalInput")
    identf_in = nc.dram_tensor("identf_in", [128, 128], F32, kind="ExternalInput")
    identb_in = nc.dram_tensor("identb_in", [128, 128], BF16, kind="ExternalInput")
    jrev_in = nc.dram_tensor("jrev_in", [128, 128], F32, kind="ExternalInput")
    onesr_in = nc.dram_tensor("onesr_in", [1, 128], BF16, kind="ExternalInput")
    onesrf_in = nc.dram_tensor("onesrf_in", [1, 128], F32, kind="ExternalInput")
    onesc_in = nc.dram_tensor("onesc_in", [128, 1], BF16, kind="ExternalInput")
    sumw_in = nc.dram_tensor("sumw_in", [128, 128], BF16, kind="ExternalInput")
    repl_in = nc.dram_tensor("repl_in", [64, 1024], BF16, kind="ExternalInput")

    y_out = nc.dram_tensor("y_out", [NS1, 128, NBLK * F1], F32, kind="ExternalOutput")

    xsp_f = nc.dram_tensor("xsp_f", [NS1, 128, NBLK * F1], F32, kind="Internal")
    xsp_b = nc.dram_tensor("xsp_b", [NS1, 128, NBLK * F1], BF16, kind="Internal")
    kp_dram = nc.dram_tensor("kp_dram", [1, 896], F32, kind="Internal")

    off = [16512]

    def alloc(name, shape, dtype, at=None):
        a = off[0] if at is None else at
        h = nc.alloc_sbuf_tensor_at(name, shape, dtype, offset=a)
        if at is None:
            sz = int(np.prod(shape[1:])) * (4 if dtype == F32 else 2)
            off[0] = a + ((sz + 127) // 128) * 128
        return h

    cm_T = alloc("cm_T", [128, 512], BF16)
    cm_ET = alloc("cm_ET", [128, 512], BF16)
    cm_O1T = alloc("cm_O1T", [128, 512], BF16)
    cm_MtT = alloc("cm_MtT", [128, 128], BF16)
    identf = alloc("identf", [128, 128], F32)
    identb = alloc("identb", [128, 128], BF16)
    jrev = alloc("jrev", [128, 128], F32)
    onesr = alloc("onesr", [1, 128], BF16)
    onesrf = alloc("onesrf", [1, 128], F32)
    onesc = alloc("onesc", [128, 1], BF16)
    sumw = alloc("sumw", [128, 128], BF16)
    state = alloc("state", [128, 1024], BF16)
    Rb = alloc("Rb", [128, 1024], F32)
    wrt_b = alloc("wrt_b", [128, NBLK * 66], BF16)
    negmr = alloc("negmr", [1, 1024], BF16)
    nm2 = alloc("nm2", [1, 1024], BF16)
    A1row = alloc("A1row", [1, 1024], BF16)
    W1row = alloc("W1row", [1, 64], BF16)
    C1_OFF = off[0]
    c1row = alloc("c1row", [1, DM], BF16)
    SV_OFF = off[0]
    svec = alloc("svec", [1, 1536], F32)
    rghx = nc.alloc_sbuf_tensor_at("rghx", [128, 1024], BF16, offset=C1_OFF)
    ghx = nc.alloc_sbuf_tensor_at("ghx", [128, 1024], BF16, offset=SV_OFF)
    sc_sb = alloc("sc_sb", [64, 512], F32)
    sT_sb = alloc("sT_sb", [128, 64], F32)
    m8a = alloc("m8a", [128, 8], F32)
    m8b = alloc("m8b", [128, 8], F32)
    e_sb = alloc("e_sb", [128, 64], F32)
    mask_sb = alloc("mask_sb", [128, 64], F32)
    z_sb = alloc("z_sb", [128, 4], F32)
    gt_f = alloc("gt_f", [128, 64], F32)
    r2c = alloc("r2c", [128, 8], F32)
    grep = alloc("grep", [128, 1024], BF16)
    SQS_OFF = off[0]
    sqs = alloc("sqs", [128, 1024], BF16)
    gtf4 = nc.alloc_sbuf_tensor_at("gtf4", [128, 256], F32, offset=SQS_OFF)
    dcol = alloc("dcol", [128, 4], F32)
    htl_b = alloc("htl_b", [128, 4 * NTOK], BF16)

    P_END = off[0]
    RW = ((P_END + 1023) // 1024) * 1024
    assert RW + 131072 + 16384 <= 229344, f"sbuf over: {RW}"

    U0 = alloc("U0", [128, NBLK * F1], F32, at=RW)
    U1 = alloc("U1", [128, NBLK * F1], F32, at=RW + 65536)
    At_sb = alloc("At_sb", [128, NBLK * 1024], BF16, at=RW)
    X2 = [alloc("X2_0", [128, NBLK * F1], BF16, at=RW + 65536),
          alloc("X2_1", [128, NBLK * F1], BF16, at=RW + 65536 + 32768)]
    Bt_lo = alloc("Bt_lo", [128, 4 * DM], BF16, at=RW + 65536)
    Bt_hi = alloc("Bt_hi", [128, 4 * DM], BF16, at=RW)
    XF = [alloc("XF_0", [128, NBLK * F3], F32, at=RW + 32768),
          alloc("XF_1", [128, NBLK * F3], F32, at=RW + 65536 + 32768)]
    LN1 = alloc("ln1w", [128, 8 * 512], BF16, at=RW + 131072)
    UBF = alloc("ubf", [128, 4 * 512], BF16, at=RW + 131072 + 8192)
    cvt = alloc("cvt", [128, 1024], F32, at=RW + 131072 + 12288)
    htl_a = alloc("htl_a", [128, 4 * NTOK], BF16, at=RW + 131072)

    def htl_ap(kr, lo, hi):
        t = htl_a if kr < 4 else htl_b
        k = kr % 4
        return t.ap()[:, k * NTOK + lo:k * NTOK + hi]

    PH0 = RW + 65536

    def ph0(name, shape, dtype, at):
        return nc.alloc_sbuf_tensor_at(name, shape, dtype, offset=PH0 + at)

    vrow = ph0("vrow", [1, 40 * 64], F32, 0)
    smat = ph0("smat", [64, 12 * 64], F32, 10240)
    Sch = [ph0(f"Sch{k}", [128, 128], F32, 14336 + k * 512) for k in range(13)]
    Tch = [ph0(f"Tch{k}", [128, 128], F32, 21504 + k * 512) for k in range(13)]
    Xt = [ph0(f"Xt{k}", [128, 128], F32, 28672 + k * 512) for k in range(6)]
    Vch = ph0("Vch", [128, 640], F32, 31744)
    wtv = ph0("wtv", [128, 4], F32, 34304)
    krow = ph0("krow", [1, 768], F32, 34816)
    kp_sb = ph0("kp_sb", [1, 896], F32, 38400)
    c1stage = ph0("c1stage", [128, 32], F32, 42496)

    psA = nc.alloc_psum_tensor("psA", [128, 2048], F32)
    psS = nc.alloc_psum_tensor("psS", [128, 512], F32)
    ps1 = nc.alloc_psum_tensor("ps1", [128, 512], F32)
    ps2 = nc.alloc_psum_tensor("ps2", [128, 512], F32)
    psX = nc.alloc_psum_tensor("psX", [128, 512], F32)

    P = Plan(nc)
    for _n in ("ld0", "ldg", "uld", "t_s1", "t_sq", "g_sq", "s_ub", "sx", "seq",
               "sqd", "sqg", "c1g", "c1e", "c1t", "c1s", "spf", "spb", "v_ln",
               "s_st", "t_sc", "t_cv", "v_tm", "v_xe", "at1", "at2", "t_h", "btl",
               "bhi", "x2h", "x2l", "t_a1", "v_a1", "s_sco", "t_sco", "v_sq",
               "gq", "t_gr", "s_gp", "v_ht", "xfl", "yd", "t_o", "v_y"):
        P.sem(_n)

    vln_base = {}
    ubc_base = {}
    st_bases = {}
    rb_cnt = {}
    tcv_after = []
    def blk_ap(buf, b, width=F1):
        return buf.ap()[:, b * width:(b + 1) * width]

    M_AP = svec.ap()[0:1, 0:512]
    R_AP = svec.ap()[0:1, 512:1024]
    V_AP = svec.ap()[0:1, 1024:1536]
    UB = [U0, U1]

    vln_base = {}
    ubc_base = {}
    st_bases = {}
    rb_cnt = {}
    tcv_after = []

    def chain(eng, fn):
        P.wait(eng, "sx")
        P.op(eng, fn, inc="sx")

    def uld_full(s):
        return 64 * (s + 1)

    def uld_q(s, b):
        return 64 * s + 16 * (b // 8 + 1)

    def emit_prep1(s):
        buf = UB[s % 2]
        base_s1 = P.count("t_s1")
        base_sq = P.count("t_sq")
        base_gq = P.count("g_sq")
        base_ub = P.count("s_ub")
        sx_prev = P.count("sx")
        steps = []
        for g8 in range(8):
            def step(g8=g8):
                for b in range(4 * g8, 4 * g8 + 4):
                    q = b % 2
                    w = b % 4
                    P.wait("scalar", "uld", uld_q(s, b))
                    if b >= 4:
                        P.wait("scalar", "t_s1", base_s1 + (b - 4) + 1)
                        P.wait("scalar", "g_sq", base_gq + (b - 4) + 1)
                    P.op("scalar", lambda e, b=b, w=w: e.copy(
                        UBF.ap()[:, w * 512:(w + 1) * 512], blk_ap(buf, b)), inc="s_ub")
                    P.wait("tensor", "s_ub", base_ub + b + 1)
                    if b == 0:
                        P.wait("tensor", "sx", sx_prev)
                    P.op("tensor", lambda e, b=b, w=w: e.matmul(
                        ps1.ap()[:, :], sumw.ap()[:, :],
                        UBF.ap()[:, w * 512:(w + 1) * 512],
                        start=(b == 0), stop=(b == NBLK - 1),
                        skip_group_check=True), inc="t_s1")
                    P.wait("gpsimd", "s_ub", base_ub + b + 1)
                    if b >= 2:
                        P.wait("gpsimd", "t_sq", base_sq + (b - 1))
                    P.op("gpsimd", lambda e, b=b, q=q, w=w: e.tensor_mul(
                        sqs.ap()[:, q * 512:(q + 1) * 512],
                        UBF.ap()[:, w * 512:(w + 1) * 512],
                        UBF.ap()[:, w * 512:(w + 1) * 512]), inc="g_sq")
                    P.wait("tensor", "g_sq", base_gq + b + 1)
                    P.op("tensor", lambda e, b=b, q=q: e.matmul(
                        ps2.ap()[:, :], sumw.ap()[:, :],
                        sqs.ap()[:, q * 512:(q + 1) * 512],
                        start=(b == 0), stop=(b == NBLK - 1),
                        skip_group_check=True), inc="t_sq")
            steps.append(step)
        return steps, base_s1, base_sq

    def emit_chain1(s, base_s1, base_sq):
        sl = s % 2
        P.wait("vector", "t_s1", base_s1 + NBLK)
        P.wait("vector", "t_sq", base_sq + NBLK)
        chain("vector", lambda e: e.tensor_scalar(M_AP, ps1.ap()[0:1, :], 1.0 / DM, None, ALU.mult))
        chain("vector", lambda e: e.tensor_scalar(V_AP, ps2.ap()[0:1, :], 1.0 / DM, None, ALU.mult))
        chain("vector", lambda e: e.tensor_mul(R_AP, M_AP, M_AP))
        chain("vector", lambda e: e.tensor_sub(V_AP, V_AP, R_AP))
        chain("vector", lambda e: e.tensor_scalar(V_AP, V_AP, EPS, None, ALU.add))
        chain("scalar", lambda e: e.activation(R_AP, V_AP, AF.Ln))
        chain("scalar", lambda e: e.activation(R_AP, R_AP, AF.Exp, scale=-0.5))
        chain("vector", lambda e, sl=sl: e.tensor_scalar(negmr.ap()[0:1, sl * 512:(sl + 1) * 512], M_AP, -1.0, None, ALU.mult))
        chain("tensor", lambda e: e.matmul(psX.ap()[:, :], onesrf.ap()[:, :], R_AP, start=True, stop=True))
        chain("scalar", lambda e, sl=sl: e.copy(Rb.ap()[:, sl * 512:(sl + 1) * 512], psX.ap()[:, :]))
        rb_cnt[s] = P.count("sx")


    def bload(fn):
        P.dma("sync", fn, "ld0", serialize=False)

    def gload(fn):
        P.dma("gpsimd", fn, "ldg", serialize=False)

    def sd(fn):
        P.wait("sync", "seq")
        P.wait("sync", "sqd")
        P.wait("sync", "sqg")
        P.dma("sync", fn, "sqd")

    def gd(fn):
        P.wait("gpsimd", "seq")
        P.wait("gpsimd", "sqd")
        P.wait("gpsimd", "sqg")
        P.dma("gpsimd", fn, "sqg")

    PRE0 = []
    SO_CTR = [0]

    def so(eng, fn):
        P.wait(eng, "seq")
        P.wait(eng, "sqd")
        P.wait(eng, "sqg")
        P.op(eng, fn, inc="seq")
        SO_CTR[0] += 1
        if SO_CTR[0] % 5 == 0 and SO_CTR[0] <= 45 and PRE0:
            PRE0.pop(0)()

    bload(lambda e: e.dma_start(out=identf.ap(), in_=identf_in.ap()))
    bload(lambda e: e.dma_start(out=identb.ap(), in_=identb_in.ap()))
    bload(lambda e: e.dma_start(out=jrev.ap(), in_=jrev_in.ap()))
    bload(lambda e: e.dma_start(out=onesr.ap(), in_=onesr_in.ap()))
    bload(lambda e: e.dma_start(out=onesrf.ap(), in_=onesrf_in.ap()))
    bload(lambda e: e.dma_start(out=onesc.ap(), in_=onesc_in.ap()))
    bload(lambda e: e.dma_start(out=sumw.ap(), in_=sumw_in.ap()))
    bload(lambda e: e.dma_start(out=rghx.ap()[64:128, :], in_=repl_in.ap()))

    VR = {}

    def vs(name):
        if name not in VR:
            VR[name] = len(VR)
            assert len(VR) <= 36
        i = VR[name]
        return vrow.ap()[0:1, i * 64:(i + 1) * 64]

    SCD = {}

    def ss(name):
        if name not in SCD:
            SCD[name] = len(SCD)
            assert len(SCD) <= 60
        i = SCD[name]
        return vrow.ap()[0:1, 36 * 64 + i:36 * 64 + i + 1]

    for nm, src, row in [("lr", lam2, 0), ("li", lam2, 1), ("pr", p2, 0),
                         ("pi", p2, 1), ("qr", q2, 0), ("qi", q2, 1),
                         ("cr", ct2, 0), ("ci", ct2, 1)]:
        bload(lambda e, d=vs(nm), s=src, r=row: e.dma_start(out=d, in_=s.ap()[r:r + 1, :]))
    bload(lambda e: e.dma_start(out=smat.ap()[0:64, 0:64], in_=vctr.ap()))
    bload(lambda e: e.dma_start(out=smat.ap()[0:64, 64:128], in_=vcti.ap()))
    bload(lambda e: e.dma_start(out=wtv.ap()[0:64, 2:3], in_=bp_in.ap()))
    bload(lambda e: e.dma_start(out=ss("D"), in_=dd_in.ap()))
    bload(lambda e: e.dma_start(out=ss("ls"), in_=ls_in.ap()))
    for db in range(NBLK):
        gload(lambda e, db=db: e.dma_start(out=wrt_b.ap()[:, 66 * db:66 * (db + 1)],
                                           in_=wrt_in.ap()[128 * db:128 * (db + 1), :]))

    ONES64 = vs("o64")
    for eng in ("vector", "scalar", "tensor"):
        P.wait(eng, "ld0", P.count("ld0"))
    for eng in ("sync", "gpsimd"):
        P.wait(eng, "ld0", P.count("ld0"))
        P.wait(eng, "ldg", P.count("ldg"))
    LDG_ALL = P.count("ldg")

    for q in range(4):
        P.dma("sync", lambda e, q=q: e.dma_start(
            out=UB[0].ap()[:, q * 4096:(q + 1) * 4096],
            in_=bass.AP(u_in, q * 4096, [[16384, 128], [1, 4096]])), "uld")
    _steps0, _b1, _b2 = emit_prep1(0)
    PRE0.extend(_steps0)

    so("vector", lambda e: e.memset(ONES64, 1.0))
    ONE1 = onesrf.ap()[0:1, 0:1]

    so("scalar", lambda e: e.activation(ss("step"), ss("ls"), AF.Exp))
    so("vector", lambda e: e.reciprocal(ss("istep"), ss("step")))
    so("vector", lambda e: e.tensor_scalar(ss("c0"), ss("istep"), 2.0, None, ALU.mult))
    so("vector", lambda e: e.tensor_scalar(ss("c4"), ss("istep"), 4.0, None, ALU.mult))

    def vb(out, a, b, op):
        so("vector", lambda e: e.tensor_tensor(out, a, b, op))

    def vsc(out, a, s, op):
        so("vector", lambda e: e.tensor_scalar(out, a, s, None, op))

    T1, T2, T3, T4 = vs("t1"), vs("t2"), vs("t3"), vs("t4")

    def cmul(outr, outi, ar, ai, br, bi):
        vb(T1, ar, br, ALU.mult)
        vb(T2, ai, bi, ALU.mult)
        vb(T3, ar, bi, ALU.mult)
        vb(T4, ai, br, ALU.mult)
        vb(outr, T1, T2, ALU.subtract)
        vb(outi, T3, T4, ALU.add)

    D0r, D0i = vs("D0r"), vs("D0i")
    AR, DEN = vs("arr"), vs("den")
    vsc(AR, vs("lr"), -1.0, ALU.mult)
    so("vector", lambda e: e.tensor_scalar(AR, AR, ss("c0"), None, ALU.add))
    vb(DEN, AR, AR, ALU.mult)
    vb(T1, vs("li"), vs("li"), ALU.mult)
    vb(DEN, DEN, T1, ALU.add)
    so("vector", lambda e: e.reciprocal(vs("rden"), DEN))
    vb(D0r, AR, vs("rden"), ALU.mult)
    vb(D0i, vs("li"), vs("rden"), ALU.mult)

    DPr, DPi, QDr, QDi = vs("dpr"), vs("dpi"), vs("qdr"), vs("qdi")
    cmul(DPr, DPi, D0r, D0i, vs("pr"), vs("pi"))
    NQI = vs("nqi")
    vsc(NQI, vs("qi"), -1.0, ALU.mult)
    cmul(QDr, QDi, vs("qr"), NQI, D0r, D0i)
    SRv, SIv = vs("srv"), vs("siv")
    cmul(SRv, SIv, QDr, QDi, vs("pr"), vs("pi"))
    so("vector", lambda e: e.reduce_sum(ss("dAr"), SRv, axis=AX))
    so("vector", lambda e: e.reduce_sum(ss("dAi"), SIv, axis=AX))
    so("vector", lambda e: e.tensor_scalar(ss("dAr"), ss("dAr"), 1.0, None, ALU.add))
    vb(ss("dd2"), ss("dAr"), ss("dAr"), ALU.mult)
    vb(ss("tt"), ss("dAi"), ss("dAi"), ALU.mult)
    vb(ss("dd2"), ss("dd2"), ss("tt"), ALU.add)
    so("vector", lambda e: e.reciprocal(ss("rdd"), ss("dd2")))
    vb(ss("sgr"), ss("dAr"), ss("rdd"), ALU.mult)
    vb(ss("sgi"), ss("dAi"), ss("rdd"), ALU.mult)
    vsc(ss("sgi"), ss("sgi"), -1.0, ALU.mult)

    ORm = smat.ap()[0:64, 2 * 64:3 * 64]
    OIm = smat.ap()[0:64, 3 * 64:4 * 64]
    NDPi = vs("ndpi")
    vsc(NDPi, DPi, -1.0, ALU.mult)
    so("tensor", lambda e: e.matmul(psA.ap()[0:64, 0:64], DPr, QDr, start=True, stop=False))
    so("tensor", lambda e: e.matmul(psA.ap()[0:64, 0:64], NDPi, QDi, start=False, stop=True))
    so("tensor", lambda e: e.matmul(psA.ap()[0:64, 64:128], DPr, QDi, start=True, stop=False))
    so("tensor", lambda e: e.matmul(psA.ap()[0:64, 64:128], DPi, QDr, start=False, stop=True))
    so("vector", lambda e: e.tensor_copy(ORm, psA.ap()[0:64, 0:64]))
    so("vector", lambda e: e.tensor_copy(OIm, psA.ap()[0:64, 64:128]))
    SGRc = dcol.ap()[0:64, 0:1]
    SGIc = dcol.ap()[0:64, 1:2]
    so("tensor", lambda e: e.matmul(psX.ap()[0:64, 0:1], ONES64, ss("sgr"), start=True, stop=True))
    so("vector", lambda e: e.tensor_copy(SGRc, psX.ap()[0:64, 0:1]))
    so("tensor", lambda e: e.matmul(psX.ap()[0:64, 0:1], ONES64, ss("sgi"), start=True, stop=True))
    so("vector", lambda e: e.tensor_copy(SGIc, psX.ap()[0:64, 0:1]))
    SOr = smat.ap()[0:64, 4 * 64:5 * 64]
    SOi = smat.ap()[0:64, 5 * 64:6 * 64]
    TMm = smat.ap()[0:64, 6 * 64:7 * 64]
    so("vector", lambda e: e.tensor_scalar(SOr, ORm, SGRc, None, ALU.mult))
    so("vector", lambda e: e.tensor_scalar(TMm, OIm, SGIc, None, ALU.mult))
    vb(SOr, SOr, TMm, ALU.subtract)
    so("vector", lambda e: e.tensor_scalar(SOi, ORm, SGIc, None, ALU.mult))
    so("vector", lambda e: e.tensor_scalar(TMm, OIm, SGRc, None, ALU.mult))
    vb(SOi, SOi, TMm, ALU.add)
    DGr = smat.ap()[0:64, 7 * 64:8 * 64]
    DGi = smat.ap()[0:64, 8 * 64:9 * 64]
    so("tensor", lambda e: e.matmul(psX.ap()[0:64, 0:64], ONES64, D0r, start=True, stop=True))
    so("vector", lambda e: e.tensor_tensor(DGr, psX.ap()[0:64, 0:64], identf.ap()[0:64, 0:64], ALU.mult))
    so("tensor", lambda e: e.matmul(psX.ap()[0:64, 0:64], ONES64, D0i, start=True, stop=True))
    so("vector", lambda e: e.tensor_tensor(DGi, psX.ap()[0:64, 0:64], identf.ap()[0:64, 0:64], ALU.mult))
    A1r = smat.ap()[0:64, 9 * 64:10 * 64]
    A1i = smat.ap()[0:64, 10 * 64:11 * 64]
    NA1i = smat.ap()[0:64, 11 * 64:12 * 64]
    vb(A1r, DGr, SOr, ALU.subtract)
    vb(A1i, DGi, SOi, ALU.subtract)
    vsc(NA1i, A1i, -1.0, ALU.mult)

    RA1, RA1T = Xt[4], Xt[5]
    so("vector", lambda e: e.tensor_copy(RA1.ap()[0:64, 0:64], A1r))
    so("vector", lambda e: e.tensor_copy(RA1.ap()[0:64, 64:128], NA1i))
    so("vector", lambda e: e.tensor_copy(RA1.ap()[64:128, 0:64], A1i))
    so("vector", lambda e: e.tensor_copy(RA1.ap()[64:128, 64:128], A1r))
    so("tensor", lambda e: e.transpose(psA.ap()[:, 0:128], RA1.ap()[:, :], identf.ap()[:, :]))
    so("vector", lambda e: e.tensor_copy(RA1T.ap()[:, :], psA.ap()[:, 0:128]))

    C4c = dcol.ap()[:, 2:3]
    so("tensor", lambda e: e.matmul(psX.ap()[:, 0:1], onesrf.ap()[:, :], ss("c4"), start=True, stop=True))
    so("vector", lambda e: e.tensor_copy(C4c, psX.ap()[:, 0:1]))
    so("vector", lambda e: e.tensor_scalar(Sch[0].ap()[:, :], RA1.ap()[:, :], C4c, None, ALU.mult))
    vb(Sch[0].ap()[:, :], Sch[0].ap()[:, :], identf.ap()[:, :], ALU.subtract)
    so("vector", lambda e: e.tensor_scalar(Tch[0].ap()[:, :], RA1T.ap()[:, :], C4c, None, ALU.mult))
    vb(Tch[0].ap()[:, :], Tch[0].ap()[:, :], identf.ap()[:, :], ALU.subtract)

    BH = wtv.ap()[:, 1:2]
    so("tensor", lambda e: e.matmul(psX.ap()[0:64, 1:2], smat.ap()[0:64, 0:64], wtv.ap()[0:64, 2:3], start=True, stop=True))
    so("tensor", lambda e: e.matmul(psX.ap()[64:128, 1:2], smat.ap()[0:64, 64:128], wtv.ap()[0:64, 2:3], start=True, stop=True))
    so("vector", lambda e: e.tensor_copy(wtv.ap()[:, 3:4], psX.ap()[:, 1:2]))
    so("tensor", lambda e: e.matmul(psX.ap()[:, 1:2], RA1T.ap()[:, :], wtv.ap()[:, 3:4], start=True, stop=True))
    so("vector", lambda e: e.tensor_scalar(BH, psX.ap()[:, 1:2], 2.0, None, ALU.mult))

    for k in range(1, 13):
        so("tensor", lambda e, k=k: e.matmul(psA.ap()[:, 0:128], Tch[k - 1].ap()[:, :], Sch[k - 1].ap()[:, :], start=True, stop=True))
        so("vector", lambda e, k=k: e.tensor_copy(Sch[k].ap()[:, :], psA.ap()[:, 0:128]))
        so("tensor", lambda e, k=k: e.matmul(psA.ap()[:, 128:256], Sch[k - 1].ap()[:, :], Tch[k - 1].ap()[:, :], start=True, stop=True))
        so("vector", lambda e, k=k: e.tensor_copy(Tch[k].ap()[:, :], psA.ap()[:, 128:256]))

    vb(Xt[0].ap()[:, :], Sch[12].ap()[:, :], identf.ap()[:, :], ALU.add)
    vb(Xt[1].ap()[:, :], Tch[12].ap()[:, :], identf.ap()[:, :], ALU.add)
    for j in range(4):
        so("tensor", lambda e: e.matmul(psA.ap()[:, 0:128], Tch[12].ap()[:, :], Sch[12].ap()[:, :], start=True, stop=True))
        so("tensor", lambda e: e.matmul(psA.ap()[:, 128:256], Sch[12].ap()[:, :], Tch[12].ap()[:, :], start=True, stop=True))
        so("vector", lambda e: e.tensor_copy(Sch[12].ap()[:, :], psA.ap()[:, 0:128]))
        so("vector", lambda e: e.tensor_copy(Tch[12].ap()[:, :], psA.ap()[:, 128:256]))
        vb(Xt[2].ap()[:, :], Sch[12].ap()[:, :], identf.ap()[:, :], ALU.add)
        vb(Xt[3].ap()[:, :], Tch[12].ap()[:, :], identf.ap()[:, :], ALU.add)
        so("tensor", lambda e: e.matmul(psA.ap()[:, 0:128], Xt[1].ap()[:, :], Xt[2].ap()[:, :], start=True, stop=True))
        so("tensor", lambda e: e.matmul(psA.ap()[:, 128:256], Xt[2].ap()[:, :], Xt[1].ap()[:, :], start=True, stop=True))
        so("vector", lambda e: e.tensor_copy(Xt[0].ap()[:, :], psA.ap()[:, 0:128]))
        so("vector", lambda e: e.tensor_copy(Xt[1].ap()[:, :], psA.ap()[:, 128:256]))

    CH = wtv.ap()[:, 2:3]
    so("tensor", lambda e: e.matmul(psX.ap()[0:64, 2:3], vs("cr"), ONE1, start=True, stop=True))
    so("tensor", lambda e: e.matmul(psX.ap()[64:128, 2:3], vs("ci"), ONE1, start=True, stop=True))
    so("vector", lambda e: e.tensor_copy(CH, psX.ap()[:, 2:3]))
    WT = wtv.ap()[:, 0:1]
    so("tensor", lambda e: e.matmul(psX.ap()[:, 0:1], Xt[0].ap()[:, :], CH, start=True, stop=True))
    so("vector", lambda e: e.tensor_copy(WT, psX.ap()[:, 0:1]))

    so("vector", lambda e: e.tensor_copy(Vch.ap()[:, 0:1], BH))
    w, k = 1, 0
    while w < 512:
        so("tensor", lambda e, w=w, k=k: e.matmul(psA.ap()[:, 0:512][:, 0:w], Tch[k].ap()[:, :], Vch.ap()[:, 0:w], start=True, stop=True))
        so("vector", lambda e, w=w: e.tensor_copy(Vch.ap()[:, w:2 * w], psA.ap()[:, 0:512][:, 0:w]))
        w, k = w * 2, k + 1
    so("tensor", lambda e: e.matmul(psA.ap()[:, 0:128], Tch[9].ap()[:, :], Vch.ap()[:, 0:128], start=True, stop=True))
    so("vector", lambda e: e.tensor_copy(Vch.ap()[:, 512:640], psA.ap()[:, 0:128]))

    so("tensor", lambda e: e.matmul(psS.ap()[0:1, 0:512], WT, Vch.ap()[:, 0:512], start=True, stop=True))
    so("vector", lambda e: e.tensor_copy(krow.ap()[0:1, 0:512], psS.ap()[0:1, 0:512]))
    so("tensor", lambda e: e.matmul(psS.ap()[0:1, 0:128], WT, Vch.ap()[:, 512:640], start=True, stop=True))
    so("vector", lambda e: e.tensor_copy(krow.ap()[0:1, 512:640], psS.ap()[0:1, 0:128]))
    so("vector", lambda e: e.memset(kp_sb.ap()[0:1, :], 0.0))
    so("vector", lambda e: e.tensor_copy(kp_sb.ap()[0:1, 128:768], krow.ap()[0:1, 0:640]))
    sd(lambda e: e.dma_start(out=kp_dram.ap(), in_=kp_sb.ap()))

    so("tensor", lambda e: e.matmul(psX.ap()[:, 0:1], onesrf.ap()[:, :], ss("D"), start=True, stop=True))
    so("vector", lambda e: e.tensor_copy(dcol.ap()[:, 3:4], psX.ap()[:, 0:1]))
    for d in range(4):
        src = bass.AP(kp_dram, 1 + 128 * d, [[1, 128], [1, 128]])
        sd(lambda e, s=src: e.dma_start(out=Xt[2].ap()[:, :], in_=s))
        so("tensor", lambda e: e.matmul(psA.ap()[:, 0:128], jrev.ap()[:, :], Xt[2].ap()[:, :], start=True, stop=True))
        if d == 0:
            so("vector", lambda e: e.scalar_tensor_tensor(
                cm_T.ap()[:, 0:128], identf.ap()[:, :], dcol.ap()[:, 3:4],
                psA.ap()[:, 0:128], ALU.mult, ALU.add))
        else:
            so("vector", lambda e, d=d: e.tensor_copy(cm_T.ap()[:, 128 * d:128 * (d + 1)], psA.ap()[:, 0:128]))

    for jb in range(4):
        so("tensor", lambda e, jb=jb: e.transpose(psA.ap()[:, 0:128], Vch.ap()[:, 128 * (3 - jb):128 * (4 - jb)], identf.ap()[:, :]))
        so("vector", lambda e: e.tensor_copy(Xt[2].ap()[:, :], psA.ap()[:, 0:128]))
        so("tensor", lambda e: e.matmul(psA.ap()[:, 128:256], jrev.ap()[:, :], Xt[2].ap()[:, :], start=True, stop=True))
        so("vector", lambda e, jb=jb: e.tensor_copy(cm_ET.ap()[:, 128 * jb:128 * (jb + 1)], psA.ap()[:, 128:256]))

    so("tensor", lambda e: e.matmul(psA.ap()[:, 0:1], Sch[0].ap()[:, :], WT, start=True, stop=True))
    so("vector", lambda e: e.tensor_copy(Vch.ap()[:, 0:1], psA.ap()[:, 0:1]))
    w, k = 1, 0
    while w < 512:
        so("tensor", lambda e, w=w, k=k: e.matmul(psA.ap()[:, 0:512][:, 0:w], Sch[k].ap()[:, :], Vch.ap()[:, 0:w], start=True, stop=True))
        so("vector", lambda e, w=w: e.tensor_copy(Vch.ap()[:, w:2 * w], psA.ap()[:, 0:512][:, 0:w]))
        w, k = w * 2, k + 1
    so("vector", lambda e: e.tensor_copy(cm_O1T.ap()[:, :], Vch.ap()[:, 0:512]))
    so("vector", lambda e: e.tensor_copy(cm_MtT.ap()[:, :], Tch[9].ap()[:, :]))

    while PRE0:
        PRE0.pop(0)()
    emit_chain1(0, _b1, _b2)

    ONEC = state.ap()[:, 1020:1021]
    so("vector", lambda e: e.memset(ONEC, 1.0))
    STA = state.ap()[:, 1022:1023]
    STB = state.ap()[:, 1023:1024]
    P.wait("tensor", "seq")
    P.wait("scalar", "seq")
    for c in range(NCH):
        cur = STA if c % 2 == 0 else STB
        nxt = STB if c % 2 == 0 else STA
        if c > 0:
            P.wait("tensor", "c1s", c)
        for tb in range(NSUB):
            if c > 0:
                P.wait("tensor", "c1e", 4 * (c - 1) + tb + 1)
            first = True
            if c > 0:
                P.op("tensor", lambda e, tb=tb, cur=cur: e.matmul(
                    psA.ap()[:, tb * 512:tb * 512 + 1],
                    cm_O1T.ap()[:, 128 * tb:128 * (tb + 1)], cur, start=True, stop=False))
                first = False
            for jb in range(tb + 1):
                fst = first and jb == 0
                P.op("tensor", lambda e, tb=tb, jb=jb, fst=fst: e.matmul(
                    psA.ap()[:, tb * 512:tb * 512 + 1],
                    cm_T.ap()[:, 128 * (tb - jb):128 * (tb - jb + 1)], ONEC,
                    start=fst, stop=(jb == tb)), inc=("c1g" if jb == tb else None))
                first = False
            P.wait("scalar", "c1g", 4 * c + tb + 1)
            P.op("scalar", lambda e, c=c, tb=tb: e.copy(
                c1stage.ap()[:, 4 * c + tb:4 * c + tb + 1],
                psA.ap()[:, tb * 512:tb * 512 + 1]), inc="c1e")
        for jb in range(NSUB):
            last0 = (c == 0 and jb == NSUB - 1)
            P.op("tensor", lambda e, jb=jb, last0=last0: e.matmul(
                psS.ap()[:, 0:1], cm_ET.ap()[:, 128 * jb:128 * (jb + 1)], ONEC,
                start=(jb == 0), stop=last0), inc=("c1t" if last0 else None))
        if c > 0:
            P.op("tensor", lambda e, cur=cur: e.matmul(
                psS.ap()[:, 0:1], cm_MtT.ap()[:, :], cur, start=False, stop=True), inc="c1t")
        P.wait("scalar", "c1t", c + 1)
        P.op("scalar", lambda e, nxt=nxt: e.copy(nxt, psS.ap()[:, 0:1]), inc="c1s")
    P.wait("tensor", "c1e", 32)
    P.wait("tensor", "c1s", 8)
    so("tensor", lambda e: e.transpose(psA.ap()[:, 0:128][0:32, :], c1stage.ap()[:, :], identf.ap()[:, :]))
    so("vector", lambda e: e.tensor_copy(Xt[2].ap()[0:32, :], psA.ap()[:, 0:128][0:32, :]))
    dstc1 = bass.AP(c1row, 0, [[DM, 1], [128, 32], [1, 128]])
    srcc1 = bass.AP(Xt[2], 0, [[128, 32], [1, 128]])
    gd(lambda e: e.dma_start(out=dstc1, in_=srcc1))

    PH0_DONE = P.count("seq")
    PH0_DMAD = P.count("sqd")
    PH0_DMAG = P.count("sqg")

    for s in range(1, NS1):
        if s == 1:
            P.wait("sync", "seq", PH0_DONE)
            P.wait("sync", "sqd", PH0_DMAD)
            P.wait("sync", "sqg", PH0_DMAG)
        for q in range(4):
            if s >= 2:
                P.wait("sync", "spf", 64 * (s - 2) + 16 * (q + 1))
                P.wait("sync", "spb", 64 * (s - 2) + 16 * (q + 1))
            P.dma("sync", lambda e, s=s, q=q, buf=UB[s % 2]: e.dma_start(
                out=buf.ap()[:, q * 4096:(q + 1) * 4096],
                in_=bass.AP(u_in, s * 128 * 16384 + q * 4096, [[16384, 128], [1, 4096]])), "uld")



    for s in range(NS1):
        buf = UB[s % 2]
        for eng in ("tensor", "vector", "scalar"):
            P.wait(eng, "uld", uld_full(s))
            if s == 0:
                P.wait(eng, "seq", PH0_DONE)
                P.wait(eng, "sqd", PH0_DMAD)
                P.wait(eng, "sqg", PH0_DMAG)
        nxt_steps = []
        if s + 1 < NS1:
            nxt_steps, nb1, nb2 = emit_prep1(s + 1)

        def emit_ln1(ls, c):
            lbuf = UB[ls % 2]
            G = ls * NCH + c
            for j in range(4):
                b = 4 * c + j
                wslot = (c % 2) * 4 + j
                P.wait("scalar", "uld", uld_q(ls, b))
                if G >= 2:
                    P.wait("scalar", "t_cv", tcv_after[G - 2])
                P.op("scalar", lambda e, b=b, w=wslot, lbuf=lbuf: e.copy(
                    LN1.ap()[:, w * 512:(w + 1) * 512],
                    blk_ap(lbuf, b)), inc="v_ln")

        sl = s % 2
        if s == 0:
            vln_base[0] = P.count("v_ln")
            emit_ln1(0, 0)
        vxe_b = P.count("v_xe")
        for c in range(NCH):
            if c + 1 < NCH and (s == 0 or c > 0):
                emit_ln1(s, c + 1)
            cur = state.ap()[:, (c % 2) * 512:(c % 2) * 512 + 512]
            nxt = state.ap()[:, ((c + 1) % 2) * 512:((c + 1) % 2) * 512 + 512]
            P.wait("tensor", "v_ln", vln_base[s] + 4 * (c + 1))
            P.wait("tensor", "sx", rb_cnt[s])
            P.wait("tensor", "s_st")
            for jb in range(NSUB):
                wslot = (c % 2) * 4 + jb
                last0 = (c == 0 and jb == NSUB - 1)
                P.op("tensor", lambda e, jb=jb, w=wslot, last0=last0: e.matmul(
                    psS.ap()[:, :], cm_ET.ap()[:, 128 * jb:128 * (jb + 1)],
                    LN1.ap()[:, w * 512:(w + 1) * 512],
                    start=(jb == 0), stop=last0), inc=("t_sc" if last0 else None))
            if c > 0:
                P.op("tensor", lambda e, cur=cur: e.matmul(
                    psS.ap()[:, :], cm_MtT.ap()[:, :], cur, start=False, stop=True), inc="t_sc")
            P.wait("vector", "t_sc")
            P.op("vector", lambda e, nxt=nxt: e.tensor_copy(nxt, psS.ap()[:, :]), inc="s_st")
            for tb in range(NSUB):
                gg = P.count("t_cv")
                bank = gg % 4
                if gg >= 4:
                    P.wait("tensor", "v_tm", gg - 3)
                first = True
                if c > 0:
                    P.op("tensor", lambda e, tb=tb, cur=cur, bank=bank: e.matmul(
                        psA.ap()[:, bank * 512:(bank + 1) * 512],
                        cm_O1T.ap()[:, 128 * tb:128 * (tb + 1)], cur, start=True, stop=False))
                    first = False
                for jb in range(tb + 1):
                    fst = first and jb == 0
                    wslot = (c % 2) * 4 + jb
                    P.op("tensor", lambda e, tb=tb, w=wslot, d=tb - jb, fst=fst, bank=bank: e.matmul(
                        psA.ap()[:, bank * 512:(bank + 1) * 512],
                        cm_T.ap()[:, 128 * d:128 * (d + 1)],
                        LN1.ap()[:, w * 512:(w + 1) * 512], start=fst, stop=False))
                    first = False
                tblk = 4 * c + tb
                P.op("tensor", lambda e, tblk=tblk, bank=bank, sl=sl: e.matmul(
                    psA.ap()[:, bank * 512:(bank + 1) * 512],
                    c1row.ap()[0:1, 128 * tblk:128 * (tblk + 1)],
                    negmr.ap()[0:1, sl * 512:(sl + 1) * 512], start=False, stop=True), inc="t_cv")
                b = 4 * c + tb
                P.wait("vector", "t_cv", P.count("t_cv"))
                P.wait("vector", "sx", rb_cnt[s])
                if gg >= 2:
                    P.wait("vector", "v_xe", P.count("v_xe") - 1)
                q2 = gg % 2
                P.op("vector", lambda e, bank=bank, sl=sl, q2=q2: e.tensor_mul(
                    cvt.ap()[:, q2 * 512:(q2 + 1) * 512],
                    psA.ap()[:, bank * 512:(bank + 1) * 512],
                    Rb.ap()[:, sl * 512:(sl + 1) * 512]), inc="v_tm")
                P.wait("vector", "v_tm")
                P.op("vector", lambda e, b=b, buf=buf, q2=q2: e.tensor_add(
                    blk_ap(buf, b), blk_ap(buf, b),
                    cvt.ap()[:, q2 * 512:(q2 + 1) * 512]), inc="v_xe")
            tcv_after.append(P.count("t_cv"))
            if c % 2 == 1:
                q = c // 2
                P.wait("scalar", "v_xe", vxe_b + 8 * (q + 1))
                if s == 3 and q >= 2:
                    P.wait("scalar", "at2", 16 * 16)
                P.dma("scalar", lambda e, s=s, buf=buf, q=q: e.dma_start(
                    out=bass.AP(xsp_f, s * 128 * 16384 + q * 4096, [[16384, 128], [1, 4096]]),
                    in_=buf.ap()[:, q * 4096:(q + 1) * 4096]), "spf")
                P.wait("gpsimd", "v_xe", vxe_b + 8 * (q + 1))
                if s == 3 and q >= 2:
                    P.wait("gpsimd", "at2", 16 * 16)
                P.dma("gpsimd", lambda e, s=s, buf=buf, q=q: e.dma_start(
                    out=bass.AP(xsp_b, s * 128 * 16384 + q * 4096, [[16384, 128], [1, 4096]]),
                    in_=buf.ap()[:, q * 4096:(q + 1) * 4096]), "spb")
            if 3 <= c <= 6:
                for _ in range(2):
                    if nxt_steps:
                        nxt_steps.pop(0)()
            if c == 6:
                while nxt_steps:
                    nxt_steps.pop(0)()
                if s + 1 < NS1:
                    vln_base[s + 1] = P.count("v_ln")
                    emit_ln1(s + 1, 0)
                    emit_chain1(s + 1, nb1, nb2)
            if c == 7 and s + 1 < NS1:
                emit_ln1(s + 1, 1)
            if s == 3 and c == 0:
                P.wait("gpsimd", "spf", 64 * 3)
                P.wait("gpsimd", "spb", 64 * 3)
                for db in range(16):
                    P.dma("gpsimd", lambda e, db=db: e.dma_start(
                        out=At_sb.ap()[:, db * 1024:(db + 1) * 1024],
                        in_=at_in.ap()[128 * db:128 * (db + 1), :]), "at1", serialize=False)
                for db in range(16, 32):
                    P.dma("gpsimd", lambda e, db=db: e.dma_start(
                        out=At_sb.ap()[:, db * 1024:(db + 1) * 1024],
                        in_=at_in.ap()[128 * db:128 * (db + 1), :]), "at2", serialize=False)

    P.wait("gpsimd", "t_h", 24)
    for kb in range(4):
        P.dma("gpsimd", lambda e, kb=kb: e.dma_start(
            out=Bt_lo.ap()[:, kb * DM:(kb + 1) * DM],
            in_=bt_in.ap()[128 * kb:128 * (kb + 1), :]), "btl", serialize=False)

    for s in range(NS1):
        P.wait("sync", "spb", 64 * (s + 1))
        if s == 0:
            P.wait("sync", "spf", 208)
            P.wait("sync", "spb", 208)
            P.dma("sync", lambda e: e.dma_start(
                out=X2[0].ap()[:, 0:8192],
                in_=bass.AP(xsp_b, 0, [[NBLK * F1, 128], [1, 8192]])), "x2h")
            P.wait("sync", "spf", 224)
            P.wait("sync", "spb", 224)
            P.wait("sync", "x2h", 16)
            P.dma("sync", lambda e: e.dma_start(
                out=X2[0].ap()[:, 8192:16384],
                in_=bass.AP(xsp_b, 8192, [[NBLK * F1, 128], [1, 8192]])), "x2l")
            continue
        need2 = 224 if s % 2 == 0 else 256
        P.wait("sync", "spf", need2)
        P.wait("sync", "spb", need2)
        if s >= 2:
            P.wait("sync", "t_h", 8 * (s - 1))
        P.dma("sync", lambda e, s=s: e.dma_start(out=X2[s % 2].ap()[:, :], in_=xsp_b.ap()[s]), "x2l")


    def emit_prep2(s):
        xb = X2[s % 2]
        base_sq = P.count("t_sq")
        base_vq = P.count("v_sq")
        sx_prev = P.count("sx")
        steps = []
        for g8 in range(8):
            def step(g8=g8, xb=xb):
                for b in range(4 * g8, 4 * g8 + 4):
                    q = b % 2
                    if s == 0 and b < 16:
                        P.wait("vector", "x2h", 16)
                    else:
                        P.wait("vector", "x2l", 16 * (s + 1))
                    if b == 0:
                        P.wait("vector", "sx", sx_prev)
                    if b >= 2:
                        P.wait("vector", "t_sq", base_sq + (b - 1))
                    P.op("vector", lambda e, b=b, q=q, xb=xb: e.tensor_mul(
                        sqs.ap()[:, q * 512:(q + 1) * 512], blk_ap(xb, b), blk_ap(xb, b)), inc="v_sq")
                    P.wait("tensor", "v_sq", base_vq + b + 1)
                    if b == 0:
                        P.wait("tensor", "sx", sx_prev)
                        P.wait("tensor", "v_a1", 3)
                    P.op("tensor", lambda e, b=b, q=q: e.matmul(
                        ps2.ap()[:, :], sumw.ap()[:, :], sqs.ap()[:, q * 512:(q + 1) * 512],
                        start=(b == 0), stop=(b == NBLK - 1), skip_group_check=True), inc="t_sq")
            steps.append(step)
        return steps, base_sq

    def emit_chain2(s, base_sq):
        sl = s % 2
        P.wait("vector", "t_sco")
        P.wait("vector", "t_sq", base_sq + NBLK)
        if s >= 2:
            P.wait("vector", "t_h", 8 * (s - 1))
        chain("vector", lambda e: e.tensor_scalar(M_AP, psX.ap()[64:65, :], 1.0 / DM, None, ALU.mult))
        chain("vector", lambda e: e.tensor_scalar(V_AP, ps2.ap()[0:1, :], 1.0 / DM, None, ALU.mult))
        chain("vector", lambda e: e.tensor_mul(R_AP, M_AP, M_AP))
        chain("vector", lambda e: e.tensor_sub(V_AP, V_AP, R_AP))
        chain("vector", lambda e: e.tensor_scalar(V_AP, V_AP, EPS, None, ALU.add))
        chain("scalar", lambda e: e.activation(R_AP, V_AP, AF.Ln))
        chain("scalar", lambda e: e.activation(R_AP, R_AP, AF.Exp, scale=-0.5))
        chain("vector", lambda e, sl=sl: e.tensor_scalar(nm2.ap()[0:1, sl * 512:(sl + 1) * 512], M_AP, -1.0, None, ALU.mult))
        for t in range(4):
            chain("tensor", lambda e, t=t: e.matmul(
                psA.ap()[:, 1536 + t:1537 + t], R_AP[0:1, 128 * t:128 * (t + 1)], ONE1, start=True, stop=True))
        chain("vector", lambda e, sl=sl: e.tensor_copy(r2c.ap()[:, sl * 4:(sl + 1) * 4], psA.ap()[:, 1536:1540]))
        return P.count("sx")

    def emit_score_fix(s, sx_r2v):
        sl = s % 2
        P.wait("tensor", "sx", sx_r2v)
        P.op("tensor", lambda e, sl=sl: e.matmul(
            psX.ap()[0:64, :], W1row.ap()[0:1, :], nm2.ap()[0:1, sl * 512:(sl + 1) * 512],
            start=False, stop=True, skip_group_check=True), inc="t_sco")
        P.wait("scalar", "t_sco")
        P.op("scalar", lambda e: e.copy(sc_sb.ap()[:, :], psX.ap()[0:64, :]), inc="s_sco")

    def emit_scores(s):
        xb = X2[s % 2]
        if s == 0:
            P.wait("tensor", "x2h", 16)
            P.wait("tensor", "v_a1", 1)
        else:
            P.wait("tensor", "x2l", 16 * (s + 1))
        P.wait("tensor", "sx")
        P.wait("tensor", "s_sco")
        for db in range(NBLK):
            if s == 0 and db == 16:
                P.wait("tensor", "x2l", 16)
            P.op("tensor", lambda e, db=db, xb=xb: e.matmul(
                psX.ap()[0:65, :], wrt_b.ap()[:, db * 66:db * 66 + 65], blk_ap(xb, db),
                start=(db == 0), stop=(db == NBLK - 1), skip_group_check=True),
                inc=("t_sco" if db == NBLK - 1 else None))

    def gch(eng, fn):
        P.wait(eng, "gq")
        P.op(eng, fn, inc="gq")

    def emit_gating_pre(s):
        P.wait("tensor", "s_sco")
        for t in range(4):
            gch("tensor", lambda e, t=t: e.transpose(
                psA.ap()[:, 1536 + 64 * t:1600 + 64 * t],
                sc_sb.ap()[0:64, 128 * t:128 * (t + 1)], identf.ap()[0:64, 0:64]))

    def emit_gating_chain(s, t):
        sl = s % 2
        gch("vector", lambda e, t=t, sl=sl: e.tensor_scalar(
            sT_sb.ap()[:, :], psA.ap()[:, 1536 + 64 * t:1600 + 64 * t],
            r2c.ap()[:, sl * 4 + t:sl * 4 + t + 1], None, ALU.mult))
        gch("vector", lambda e: e.max(m8a.ap()[:, :], sT_sb.ap()[:, :]))
        gch("vector", lambda e: e.match_replace(gt_f.ap()[:, :], m8a.ap()[:, :], sT_sb.ap()[:, :], -1e30))
        gch("vector", lambda e: e.max(m8b.ap()[:, :], gt_f.ap()[:, :]))
        gch("scalar", lambda e: e.mul(z_sb.ap()[:, 2:3], m8a.ap()[:, 0:1], -1.0))
        gch("scalar", lambda e: e.activation(e_sb.ap()[:, :], sT_sb.ap()[:, :], AF.Exp, bias=z_sb.ap()[:, 2:3], scale=1.0))
        gch("vector", lambda e: e.tensor_scalar(mask_sb.ap()[:, :], sT_sb.ap()[:, :], m8b.ap()[:, 7:8], None, ALU.is_ge))
        gch("vector", lambda e: e.tensor_mul(e_sb.ap()[:, :], e_sb.ap()[:, :], mask_sb.ap()[:, :]))
        gch("vector", lambda e: e.reduce_sum(z_sb.ap()[:, 0:1], e_sb.ap()[:, :], axis=AX))
        gch("vector", lambda e: e.reciprocal(z_sb.ap()[:, 1:2], z_sb.ap()[:, 0:1]))
        gch("vector", lambda e, t=t, sl=sl: e.tensor_scalar(
            gtf4.ap()[:, 64 * t:64 * (t + 1)], e_sb.ap()[:, :], z_sb.ap()[:, 1:2],
            r2c.ap()[:, sl * 4 + t:sl * 4 + t + 1], ALU.mult, ALU.mult))

    def emit_gating_post(s):
        sl = s % 2
        for t in range(4):
            q = t % 2
            gch("tensor", lambda e, t=t, q=q: e.transpose(
                psA.ap()[:, 1792 + 128 * q:1920 + 128 * q][0:64, :],
                gtf4.ap()[:, 64 * t:64 * (t + 1)], identf.ap()[:, :]))
            gch("scalar", lambda e, t=t, q=q, sl=sl: e.copy(
                ghx.ap()[64:128, sl * 512 + 128 * t:sl * 512 + 128 * (t + 1)],
                psA.ap()[:, 1792 + 128 * q:1920 + 128 * q][0:64, :]))

    P.wait("tensor", "ldg", LDG_ALL)
    P.wait("tensor", "sx")
    for db in range(NBLK):
        P.op("tensor", lambda e, db=db: e.matmul(
            ps2.ap()[:, 0:64], sumw.ap()[:, :],
            wrt_b.ap()[:, db * 66:db * 66 + 64], start=(db == 0), stop=(db == NBLK - 1),
            skip_group_check=True))
    P.wait("tensor", "at1", 16 * 16)
    for db in range(16):
        P.op("tensor", lambda e, db=db: e.matmul(
            psX.ap()[0:128, :], sumw.ap()[:, :],
            At_sb.ap()[:, db * 1024:db * 1024 + 512], start=(db == 0), stop=False,
            skip_group_check=True))
        P.op("tensor", lambda e, db=db: e.matmul(
            ps1.ap()[:, :], sumw.ap()[:, :],
            At_sb.ap()[:, db * 1024 + 512:db * 1024 + 1024], start=(db == 0), stop=False,
            skip_group_check=True))
    P.wait("tensor", "at2", 16 * 16)
    for db in range(16, NBLK):
        P.op("tensor", lambda e, db=db: e.matmul(
            psX.ap()[0:128, :], sumw.ap()[:, :],
            At_sb.ap()[:, db * 1024:db * 1024 + 512], start=False, stop=(db == NBLK - 1),
            skip_group_check=True))
        P.op("tensor", lambda e, db=db: e.matmul(
            ps1.ap()[:, :], sumw.ap()[:, :],
            At_sb.ap()[:, db * 1024 + 512:db * 1024 + 1024], start=False, stop=(db == NBLK - 1),
            skip_group_check=True),
            inc=("t_a1" if db == NBLK - 1 else None))
    P.wait("vector", "t_a1")
    P.op("vector", lambda e: e.tensor_copy(A1row.ap()[0:1, 0:512], psX.ap()[0:1, :]), inc="v_a1")
    P.op("vector", lambda e: e.tensor_copy(A1row.ap()[0:1, 512:1024], ps1.ap()[0:1, :]), inc="v_a1")
    P.op("vector", lambda e: e.tensor_copy(W1row.ap()[0:1, :], ps2.ap()[0:1, 0:64]), inc="v_a1")

    emit_scores(0)
    st0, c2_ = emit_prep2(0)
    for st in st0:
        st()
    sx_r2 = emit_chain2(0, c2_)
    emit_score_fix(0, sx_r2)


    htl_after = []
    gq_done = {}
    emit_gating_pre(0)
    for t in range(4):
        emit_gating_chain(0, t)
    emit_gating_post(0)
    gq_done[0] = P.count("gq")

    for s in range(NS1):
        sl = s % 2
        xb = X2[s % 2]
        for eng in ("tensor", "vector", "scalar"):
            P.wait(eng, "x2l", 16 * (s + 1))
        nxt_steps = []
        if s + 1 < NS1:
            nxt_steps, nb2 = emit_prep2(s + 1)

        kr_start = 0
        if s == 0:
            kr_start = 4
            for kr in range(4):
                if kr == 3:
                    P.wait("tensor", "gq", gq_done[0])
                for db in range(NBLK):
                    P.op("tensor", lambda e, kr=kr, db=db, xb=xb: e.matmul(
                        psA.ap()[:, kr * 512:(kr + 1) * 512],
                        At_sb.ap()[:, db * 1024 + kr * 128:db * 1024 + (kr + 1) * 128],
                        blk_ap(xb, db), start=(db == 0), stop=False))
                P.op("tensor", lambda e, kr=kr: e.matmul(
                    psA.ap()[:, kr * 512:(kr + 1) * 512],
                    A1row.ap()[0:1, kr * 128:(kr + 1) * 128],
                    nm2.ap()[0:1, 0:512],
                    start=False, stop=True), inc="t_h")
            for kr in range(4):
                if kr >= 1:
                    P.wait("tensor", "s_gp", kr)
                P.op("tensor", lambda e, kr=kr: e.matmul(
                    psS.ap()[:, :], rghx.ap()[64:128, kr * 128:(kr + 1) * 128],
                    ghx.ap()[64:128, 0:512], start=True, stop=True), inc="t_gr")
                P.wait("scalar", "t_gr")
                if kr >= 2:
                    P.wait("scalar", "v_ht", kr - 1)
                P.op("scalar", lambda e, kr=kr: e.copy(
                    grep.ap()[:, (kr % 2) * 512:(kr % 2 + 1) * 512],
                    psS.ap()[:, :]), inc="s_gp")
                P.wait("vector", "t_h")
                P.wait("vector", "s_gp")
                P.op("vector", lambda e, kr=kr: e.tensor_mul(
                    htl_ap(kr, 0, 512),
                    psA.ap()[:, kr * 512:(kr + 1) * 512],
                    grep.ap()[:, (kr % 2) * 512:(kr % 2 + 1) * 512]), inc="v_ht")
            for _ in range(8):
                if nxt_steps:
                    nxt_steps.pop(0)()

        for kr in range(kr_start, 8):
            pb = kr % 2
            if s * 8 + kr >= 2:
                P.wait("tensor", "v_ht", P.count("v_ht") - 1)
            for db in range(NBLK):
                P.op("tensor", lambda e, kr=kr, db=db, pb=pb, xb=xb: e.matmul(
                    psA.ap()[:, pb * 512:(pb + 1) * 512],
                    At_sb.ap()[:, db * 1024 + kr * 128:db * 1024 + (kr + 1) * 128],
                    blk_ap(xb, db), start=(db == 0), stop=False))
            P.op("tensor", lambda e, kr=kr, pb=pb, sl=sl: e.matmul(
                psA.ap()[:, pb * 512:(pb + 1) * 512],
                A1row.ap()[0:1, kr * 128:(kr + 1) * 128],
                nm2.ap()[0:1, sl * 512:(sl + 1) * 512],
                start=False, stop=True), inc="t_h")
            gslot = kr % 2
            gps = psA.ap()[:, 1024:1536] if gslot == 0 else psS.ap()[:, :]
            if s == 0 and kr == 4:
                P.wait("tensor", "v_ht", 3)
            P.wait("tensor", "gq", gq_done[s])
            if s * 8 + kr >= 2:
                P.wait("tensor", "s_gp", P.count("s_gp") - 1)
            P.op("tensor", lambda e, kr=kr, sl=sl, gps=gps: e.matmul(
                gps, rghx.ap()[64:128, kr * 128:(kr + 1) * 128],
                ghx.ap()[64:128, sl * 512:(sl + 1) * 512], start=True, stop=True), inc="t_gr")
            P.wait("scalar", "t_gr")
            if s * 8 + kr >= 2:
                P.wait("scalar", "v_ht", P.count("v_ht") - 1)
            P.op("scalar", lambda e, gslot=gslot, gps=gps: e.copy(
                grep.ap()[:, gslot * 512:(gslot + 1) * 512], gps), inc="s_gp")
            P.wait("vector", "t_h")
            P.wait("vector", "s_gp")
            P.op("vector", lambda e, kr=kr, s=s, pb=pb, gslot=gslot: e.tensor_mul(
                htl_ap(kr, s * 512, s * 512 + 512),
                psA.ap()[:, pb * 512:(pb + 1) * 512],
                grep.ap()[:, gslot * 512:(gslot + 1) * 512]), inc="v_ht")
            if s + 1 < NS1:
                if kr <= 3:
                    for _ in range(2):
                        if nxt_steps:
                            nxt_steps.pop(0)()
                elif kr == 4:
                    while nxt_steps:
                        nxt_steps.pop(0)()
                    emit_scores(s + 1)
                elif kr == 5:
                    if s == 0:
                        P.wait("tensor", "v_ht", 4)
                    sx_r2 = emit_chain2(s + 1, nb2)
                    emit_score_fix(s + 1, sx_r2)
                    emit_gating_pre(s + 1)
                elif kr == 6:
                    emit_gating_chain(s + 1, 0)
                    emit_gating_chain(s + 1, 1)
                elif kr == 7:
                    emit_gating_chain(s + 1, 2)
                    emit_gating_chain(s + 1, 3)
                    emit_gating_post(s + 1)
                    gq_done[s + 1] = P.count("gq")
        htl_after.append(P.count("v_ht"))

    P.wait("gpsimd", "t_h", 32)
    for kb in range(4, 8):
        P.dma("gpsimd", lambda e, kb=kb: e.dma_start(
            out=Bt_hi.ap()[:, (kb - 4) * DM:(kb - 3) * DM],
            in_=bt_in.ap()[128 * kb:128 * (kb + 1), :]), "bhi")
    for s2 in range(NS3):
        s1, g = s2 // 2, s2 % 2
        srcap = bass.AP(xsp_f, s1 * 128 * (NBLK * F1) + g * 8192, [[NBLK * F1, 128], [1, 8192]])
        P.wait("sync", "spf", 64 * (s1 + 1))
        P.wait("sync", "x2l", 16 * 4)
        P.wait("sync", "t_h", 8 * 4)
        if s2 >= 2:
            P.wait("sync", "yd", 64 * (s2 - 1))
        for q in range(4):
            qsrc = bass.AP(xsp_f, s1 * 128 * (NBLK * F1) + g * 8192 + q * 2048,
                           [[NBLK * F1, 128], [1, 2048]])
            P.dma("sync", lambda e, s2=s2, q=q, qsrc=qsrc: e.dma_start(
                out=XF[s2 % 2].ap()[:, q * 2048:(q + 1) * 2048], in_=qsrc), "xfl")

    for s2 in range(NS3):
        s1, g = s2 // 2, s2 % 2
        xf = XF[s2 % 2]
        if s2 == 0:
            for dd in range(4):
                if dd < 2:
                    P.wait("tensor", "v_ht", htl_after[-1] - 1 + dd)
                if dd == 0:
                    P.wait("tensor", "btl", 16 * 4)
                for kb in range(4):
                    P.op("tensor", lambda e, dd=dd, kb=kb: e.matmul(
                        psA.ap()[:, dd * 512:(dd + 1) * 512],
                        Bt_lo.ap()[:, kb * DM + dd * 128:kb * DM + (dd + 1) * 128],
                        htl_ap(kb, 0, 512), start=(kb == 0), stop=False))
            for dd in range(4):
                for kb in range(4, 8):
                    if dd == 0:
                        P.wait("tensor", "bhi", 16 * (kb - 3))
                    P.op("tensor", lambda e, dd=dd, kb=kb: e.matmul(
                        psA.ap()[:, dd * 512:(dd + 1) * 512],
                        Bt_hi.ap()[:, (kb - 4) * DM + dd * 128:(kb - 4) * DM + (dd + 1) * 128],
                        htl_ap(kb, 0, 512), start=False, stop=(kb == 7)),
                        inc=("t_o" if kb == 7 else None))
                P.wait("vector", "t_o")
                P.wait("vector", "xfl", 16)
                P.op("vector", lambda e, dd=dd, xf=xf: e.scalar_tensor_tensor(
                    xf.ap()[:, dd * 512:(dd + 1) * 512], psA.ap()[:, dd * 512:(dd + 1) * 512],
                    2.0, xf.ap()[:, dd * 512:(dd + 1) * 512], ALU.mult, ALU.add), inc="v_y")
        for dd in range(4 if s2 == 0 else 0, 16):
            d = 16 * g + dd
            pb = dd % 2
            if s2 * 16 + dd >= 2:
                P.wait("tensor", "v_y", P.count("v_y") - 1)
            for kb in range(8):
                bt = Bt_lo if kb < 4 else Bt_hi
                kbo = kb % 4
                P.op("tensor", lambda e, d=d, kbo=kbo, kb=kb, pb=pb, s1=s1, bt=bt: e.matmul(
                    psA.ap()[:, pb * 512:(pb + 1) * 512],
                    bt.ap()[:, kbo * DM + d * 128:kbo * DM + (d + 1) * 128],
                    htl_ap(kb, s1 * 512, (s1 + 1) * 512),
                    start=(kb == 0), stop=(kb == 7)), inc=("t_o" if kb == 7 else None))
            P.wait("vector", "t_o")
            P.wait("vector", "xfl", 64 * s2 + 16 * (dd // 4 + 1))
            P.op("vector", lambda e, dd=dd, pb=pb, xf=xf: e.scalar_tensor_tensor(
                xf.ap()[:, dd * 512:(dd + 1) * 512], psA.ap()[:, pb * 512:(pb + 1) * 512],
                2.0, xf.ap()[:, dd * 512:(dd + 1) * 512], ALU.mult, ALU.add), inc="v_y")
        for q in range(4):
            dstap = bass.AP(y_out, s1 * 128 * (NBLK * F1) + g * 8192 + q * 2048,
                            [[NBLK * F1, 128], [1, 2048]])
            P.wait("scalar", "v_y", 16 * s2 + 4 * (q + 1))
            P.dma("scalar", lambda e, dstap=dstap, xf=xf, q=q: e.dma_start(
                out=dstap, in_=xf.ap()[:, q * 2048:(q + 1) * 2048]), "yd")

    P.wait("scalar", "yd", 64 * NS3)
    P.wait("sync", "yd", 64 * NS3)

    with nc.Block() as blk:
        P.emit_into(blk)
    return nc


_NC_CACHE = None


def _get_nc():
    global _NC_CACHE
    if _NC_CACHE is None:
        _NC_CACHE = build_nc()
    return _NC_CACHE


def kernel(L0_attn_u, L0_attn_Lambda_ri, L0_attn_p_ri, L0_attn_q_ri,
           L0_attn_Vc_ri, L0_attn_Ct_ri, L0_attn_Bp, L0_attn_D,
           L0_attn_log_step, L0_ffn_A, L0_ffn_B, L0_ffn_Wr):
    nc = _get_nc()
    urows = np.ascontiguousarray(np.asarray(L0_attn_u, dtype=np.float32).reshape(16384, DM))

    At = np.ascontiguousarray(np.asarray(L0_ffn_A, np.float32).transpose(2, 0, 1).reshape(DM, 1024))
    Bt = np.ascontiguousarray(np.asarray(L0_ffn_B, np.float32).transpose(0, 2, 1).reshape(1024, DM))
    Wrt0 = np.asarray(L0_ffn_Wr, np.float32).T
    Wrt = np.zeros((DM, 66), np.float32)
    Wrt[:, 0:64] = Wrt0
    Wrt[:, 64] = 1.0

    shared = {
        "at_in": At, "bt_in": Bt, "wrt_in": Wrt,
        "lam2": np.ascontiguousarray(np.asarray(L0_attn_Lambda_ri, np.float32).T),
        "p2": np.ascontiguousarray(np.asarray(L0_attn_p_ri, np.float32).T),
        "q2": np.ascontiguousarray(np.asarray(L0_attn_q_ri, np.float32).T),
        "ct2": np.ascontiguousarray(np.asarray(L0_attn_Ct_ri, np.float32)[0].T),
        "vctr": np.ascontiguousarray(np.asarray(L0_attn_Vc_ri, np.float32)[:, :, 0].T),
        "vcti": np.ascontiguousarray(np.asarray(L0_attn_Vc_ri, np.float32)[:, :, 1].T),
        "bp_in": np.asarray(L0_attn_Bp, np.float32).reshape(64, 1),
        "dd_in": np.asarray(L0_attn_D, np.float32).reshape(1, 1),
        "ls_in": np.asarray(L0_attn_log_step, np.float32).reshape(1, 1),
        "identf_in": np.eye(128, dtype=np.float32),
        "identb_in": np.eye(128).astype(BF),
        "jrev_in": np.ascontiguousarray(np.eye(128, dtype=np.float32)[::-1]),
        "onesr_in": np.ones((1, 128), BF),
        "onesrf_in": np.ones((1, 128), np.float32),
        "onesc_in": np.ones((128, 1), BF),
        "sumw_in": (np.arange(128)[None, :] == 0).astype(BF) * np.ones((128, 1), BF),
        "repl_in": (np.arange(64)[:, None] == (np.arange(1024)[None, :] // 16)).astype(BF),
    }

    in_maps = []
    for cidx in range(8):
        rows = urows[cidx * NTOK:(cidx + 1) * NTOK]
        ut = rows.T.reshape(NBLK, 128, NS1, F1).transpose(2, 1, 0, 3).reshape(NS1, 128, NBLK * F1)
        d = dict(shared)
        d["u_in"] = np.ascontiguousarray(ut)
        in_maps.append(d)

    res = run_bass_kernel_spmd(nc, in_maps, core_ids=list(range(8)))

    out = np.empty((16384, DM), np.float32)
    for cidx in range(8):
        yt = res.results[cidx]["y_out"]
        yt = yt.reshape(NS1, 128, NBLK, F1).transpose(2, 1, 0, 3).reshape(DM, NTOK)
        out[cidx * NTOK:(cidx + 1) * NTOK] = yt.T
    return out.reshape(2, 8192, DM)
